# version 12 (final)
# baseline (speedup 1.0000x reference)
import os
import math
import numpy as np
import concourse.bass as bass
import concourse.mybir as mybir
from concourse.bass_utils import run_bass_kernel_spmd

F32 = mybir.dt.float32
BF16 = mybir.dt.bfloat16
I32 = mybir.dt.int32
U8 = mybir.dt.uint8
AF = mybir.ActivationFunctionType
ALU = mybir.AluOpType
AX = mybir.AxisListType

PHASE = "full"
NCORES = 8

T, D, II = 2048, 1024, 2816
NH, DK = 16, 64
NT, NTT, DC, JC = 16, 4, 8, 22
EPS_LN, EPS_AT, QMAX = 1e-5, 1e-6, 127.0
NORM = 1.0 / math.sqrt(math.sqrt(DK))

ENGINES = ("sync", "vector", "scalar", "tensor", "gpsimd")
AUDIT = {}
DMA_POOL = 8
ESEM = {"vector": "VE", "scalar": "SE", "tensor": "PE", "gpsimd": "GE"}
DSEM = {"sync": "SXD", "gpsimd": "GDD"}


class KB:
    def __init__(self, dummy_builders=None):
        self.ops = {e: [] for e in ENGINES}
        self.rw = {}
        self.bar = {e: {} for e in ENGINES}
        self.dummy = dummy_builders or {}

    def _dep_add(self, deps, eng, e2, i2):
        if e2 == eng:
            if self.ops[e2][i2]["kind"] == "c" and e2 != "gpsimd":
                return
        if deps.get(e2, -1) < i2:
            deps[e2] = i2

    SMALL = {"srow", "red1", "red2", "mv", "grow", "ksum", "bcol", "A_st",
             "B_st", "AQ", "BQ", "wsg", "habs", "lnmax", "stat6", "cst"}

    def emit(self, eng, build, reads=(), writes=(), kind="c", extra=(),
             late=False):
        if not late and any(b in self.SMALL for b in writes):
            late = True
        need_gap = 0
        n = len(self.ops[eng])
        for b in reads:
            st = self.rw.get(b)
            if st:
                i2 = st["w"].get(eng)
                if i2 is not None and self.ops[eng][i2].get("late"):
                    d = n - 1 - i2
                    if d < 2:
                        need_gap = max(need_gap, 2 - d)
        for _ in range(need_gap):
            if eng in self.dummy:
                self.ops[eng].append({"build": self.dummy[eng], "deps": {},
                                      "kind": "c"})
                n += 1
        idx = len(self.ops[eng])
        deps = {}
        for e2, i2 in self.bar[eng].items():
            self._dep_add(deps, eng, e2, i2)
        self.bar[eng] = {}
        for b in reads:
            st = self.rw.get(b)
            if st:
                for e2, i2 in st["w"].items():
                    self._dep_add(deps, eng, e2, i2)
        for b in writes:
            st = self.rw.get(b)
            if st:
                for e2, i2 in st["w"].items():
                    self._dep_add(deps, eng, e2, i2)
                for e2, i2 in st["r"].items():
                    self._dep_add(deps, eng, e2, i2)
        for e2, i2 in extra:
            self._dep_add(deps, eng, e2, i2)
        self.ops[eng].append({"build": build, "deps": deps, "kind": kind,
                              "late": late})
        for b in reads:
            self.rw.setdefault(b, {"w": {}, "r": {}})["r"][eng] = idx
        for b in writes:
            self.rw[b] = {"w": {eng: idx}, "r": {}}
        return (eng, idx)

    def mark(self, eng):
        return (eng, len(self.ops[eng]) - 1)

    def barrier(self):
        last = {e: len(self.ops[e]) - 1 for e in ENGINES if self.ops[e]}
        for e in ENGINES:
            for e2, i2 in last.items():
                if e2 != e and self.bar[e].get(e2, -1) < i2:
                    self.bar[e][e2] = i2

    def resolve(self):
        marked = {e: set() for e in ENGINES}
        for e in ENGINES:
            for op in self.ops[e]:
                for e2, i2 in op["deps"].items():
                    if self.ops[e2][i2]["kind"] == "c":
                        marked[e2].add(i2)
        for e in ENGINES:
            for i in range(len(self.ops[e]) - 1, -1, -1):
                if self.ops[e][i]["kind"] == "c":
                    marked[e].add(i)
                    break
        self.rank = {e: [0] * len(self.ops[e]) for e in ENGINES}
        self.is_marked = {e: [False] * len(self.ops[e]) for e in ENGINES}
        self.dma_rank = {e: [0] * len(self.ops[e]) for e in ENGINES}
        self.cc_rank = {e: [0] * len(self.ops[e]) for e in ENGINES}
        for e in ENGINES:
            nm = nd = ncc = 0
            for i, op in enumerate(self.ops[e]):
                if op["kind"] == "c" and i in marked[e]:
                    nm += 1
                    self.is_marked[e][i] = True
                if op["kind"] == "d":
                    nd += 1
                if op["kind"] == "cc":
                    ncc += 1
                self.rank[e][i] = nm
                self.dma_rank[e][i] = nd
                self.cc_rank[e][i] = ncc
        P = DMA_POOL
        self.waits = {e: [None] * len(self.ops[e]) for e in ENGINES}
        for e in ENGINES:
            hi = {}
            for i, op in enumerate(self.ops[e]):
                ws = []
                if op["kind"] == "d":
                    r = self.dma_rank[e][i] - 1
                    sem = f"{DSEM[e]}{r % P}"
                    val = 16 * (r // P)
                    if val > 0 and hi.get(sem, 0) < val:
                        hi[sem] = val
                        ws.append((sem, val))
                for e2, i2 in sorted(op["deps"].items()):
                    k2 = self.ops[e2][i2]["kind"]
                    if k2 == "c":
                        sem, val = ESEM[e2], self.rank[e2][i2]
                    elif k2 == "d":
                        r2 = self.dma_rank[e2][i2] - 1
                        sem, val = f"{DSEM[e2]}{r2 % P}", 16 * (r2 // P + 1)
                    else:
                        sem, val = "CC", self.cc_rank[e2][i2]
                    if val > 0 and hi.get(sem, 0) < val:
                        hi[sem] = val
                        ws.append((sem, val))
                self.waits[e][i] = ws

    def replay(self, e, engine, sems):
        P = DMA_POOL
        for i, op in enumerate(self.ops[e]):
            for sem, val in self.waits[e][i]:
                engine.wait_ge(sems[sem], val)
            instr = op["build"](engine)
            if op["kind"] == "d":
                r = self.dma_rank[e][i] - 1
                instr.then_inc(sems[f"{DSEM[e]}{r % P}"], 16)
            elif op["kind"] == "cc":
                instr.then_inc(sems["CC"], 1)
            elif self.is_marked[e][i]:
                instr.then_inc(sems[ESEM[e]], 1)
        if e in DSEM and self.ops[e]:
            nd = self.dma_rank[e][len(self.ops[e]) - 1]
            for q in range(min(P, nd)):
                tot = (nd - 1 - q) // P + 1
                engine.wait_ge(sems[f"{DSEM[e]}{q}"], 16 * tot)
        if e == "gpsimd" and self.ops[e]:
            ncc = self.cc_rank[e][len(self.ops[e]) - 1]
            if ncc:
                engine.wait_ge(sems["CC"], ncc)


def build_program():
    nc = bass.Bass("TRN2", target_bir_lowering=False)

    x_in = nc.dram_tensor("x", [T, D], F32, kind="ExternalInput").ap()
    H = {"x_in": x_in}
    for n, shp in (("wq", [D, D]), ("wk", [D, D]), ("wv", [D, D]), ("wo", [D, D]),
                   ("bq", [128, DC]), ("bk", [128, DC]),
                   ("bvr", [1, D]), ("bor", [1, D]), ("pT", [DK, DK]),
                   ("clip_x", [1, 1]), ("clip_down", [1, 1])):
        H[n] = nc.dram_tensor(n, shp, F32, kind="ExternalInput").ap()
    H["out"] = nc.dram_tensor("out", [T, D], F32, kind="ExternalOutput").ap()
    H["dbg"] = nc.dram_tensor("dbg", [2, 64], F32, kind="ExternalOutput").ap()
    H["x2s"] = nc.dram_tensor("x2s", [T, D], F32, kind="Internal").ap()
    H["wg_mine"] = nc.dram_tensor("wg_mine", [384, 1024], F32,
                                  kind="ExternalInput").ap()
    H["wu_mine"] = nc.dram_tensor("wu_mine", [384, 1024], F32,
                                  kind="ExternalInput").ap()
    H["wd_mine"] = nc.dram_tensor("wd_mine", [128, II], F32,
                                  kind="ExternalInput").ap()
    for i in range(3):
        H[f"myg{i}"] = nc.dram_tensor(f"myg{i}", [256, 1024], BF16,
                                      kind="Internal").ap()
        H[f"allg{i}"] = nc.dram_tensor(f"allg{i}", [2048, 1024], BF16,
                                       kind="Internal", addr_space="Shared").ap()
    H["mywd"] = nc.dram_tensor("mywd", [II, 128], BF16, kind="Internal").ap()
    H["allwd"] = nc.dram_tensor("allwd", [8 * II, 128], BF16, kind="Internal",
                                addr_space="Shared").ap()
    H["wdT_h"] = nc.dram_tensor("wdT_h", [JC * 128, 1024], BF16,
                                kind="Internal").ap()
    H["mysc"] = nc.dram_tensor("mysc", [128, 8], F32, kind="Internal").ap()
    H["allsc"] = nc.dram_tensor("allsc", [1024, 8], F32, kind="Internal",
                                addr_space="Shared").ap()
    H["cc_in"] = [nc.dram_tensor(f"cc_in{i}", [1, 1], F32, kind="Internal").ap()
                  for i in range(4)]
    H["cc_out"] = [nc.dram_tensor(f"cc_out{i}", [1, 8], F32, kind="Internal",
                                  addr_space="Shared").ap()
                   for i in range(4)]

    AB = 212480
    OFF = {}
    _o = [0]

    def rsv(name, nbytes):
        OFF[name] = _o[0]
        _o[0] += nbytes
        assert _o[0] <= AB, (name, _o[0])

    rsv("x_res", 65536)
    rsv("actA", 32768)
    rsv("harena", 90112)
    rsv("i32", 4096)
    rsv("qscr", 4096)
    rsv("silu", 2048)
    for n, b in (("ident_bf", 256), ("ident_f32", 512), ("onesf", 512),
                 ("onesb", 256), ("onecb", 8), ("A_st", 64), ("B_st", 64), ("AQ", 64),
                 ("BQ", 64), ("lnmax", 64), ("habs", 352), ("wsg", 352),
                 ("stat6", 48), ("mv", 8), ("red1", 4), ("red2", 4),
                 ("bcol", 336), ("wsga", 256), ("srow", 256), ("grow", 32), ("bq_sb", 32),
                 ("bk_sb", 32), ("pT_f32", 256), ("pnT", 128), ("clip_sb", 8),
                 ("ksum", 32), ("xin0", 4096), ("xin1", 4096), ("cst", 32)):
        rsv(n, b)
    XR, HA, XA = OFF["x_res"], OFF["harena"], OFF["actA"]
    for n, off in (("wgT0", 16384), ("wgT1", 18432),
                   ("wuT0", 20480), ("wuT1", 22528)):
        OFF[n] = XR + off
    for n, off in (("pnat0", 0), ("pnat1", 4096), ("pwdn", 8192),
                   ("pq", 19456), ("pwdq", 19456), ("pstg0", 21504),
                   ("pstg1", 23552), ("pwdt", 0)):
        OFF[n] = XA + off
    for n, off in (("wA", 0), ("wB", 16384), ("k_scr", 32768), ("kf_scr", 40960),
                   ("kftm", 49152), ("vtm", 57344), ("attn_scr", 65536),
                   ("rden", 73728), ("bd", 75776), ("kv_sb", 77824),
                   ("bvrow", 78848), ("borow", 80896), ("wnat_a", 82944)):
        OFF[n] = HA + off
    assert OFF["wnat_a"] + 4096 <= HA + 90112

    V = {}

    def mkviews(base):
        def mk(name, shape, dt, off):
            h = nc.virtual_sbuf_tensor(name, shape, dt)
            nc._tensor(h.name, h.shape, h.dtype, type="SB")
            ml = nc.lookup_mloc(h)
            ml.allocated = True
            ml.addr = base + off
            AUDIT[h.name] = off
            V[name] = h.ap()

        mk("x_res", [128, 16 * 1024], F32, OFF["x_res"])
        mk("actA", [128, 8 * 2048], BF16, OFF["actA"])
        mk("harena", [128, 22 * 2048], BF16, OFF["harena"])
        mk("i32", [128, 1024], I32, OFF["i32"])
        mk("qscr", [128, 1024], F32, OFF["qscr"])
        mk("silu", [128, 512], F32, OFF["silu"])
        for n in ("wgT0", "wgT1", "wuT0", "wuT1"):
            mk(n, [128, 1024], BF16, OFF[n])
        for n in ("pnat0", "pnat1"):
            mk(n, [128, 1024], F32, OFF[n])
        mk("pwdn", [128, II], F32, OFF["pwdn"])
        mk("pq", [128, 1024], BF16, OFF["pq"])
        mk("pwdq", [128, II], BF16, OFF["pwdq"])
        for n in ("pstg0", "pstg1"):
            mk(n, [128, 1024], BF16, OFF[n])
        mk("pwdt", [128, II], BF16, OFF["pwdt"])
        mk("silu2", [128, 512], F32, XR + 0)
        for n in ("wA", "wB"):
            mk(n, [128, 8192], BF16, OFF[n])
        for n in ("k_scr", "kf_scr", "kftm", "vtm", "attn_scr"):
            mk(n, [128, 4096], BF16, OFF[n])
        mk("astage", [128, 1024], BF16, OFF["vtm"])
        mk("astage2", [128, 1024], BF16, OFF["vtm"] + 2048)
        mk("rden", [128, 512], F32, OFF["rden"])
        mk("wnat_a", [128, 1024], F32, OFF["wnat_a"])
        mk("wnat_b", [128, 1024], F32, OFF["k_scr"])
        mk("wdT16", [128, 1024], BF16, OFF["qscr"])
        mk("wdT17", [128, 1024], BF16, OFF["qscr"] + 2048)
        mk("wdT18", [128, 1024], BF16, OFF["i32"])
        mk("wdT19", [128, 1024], BF16, OFF["i32"] + 2048)
        mk("wdT20", [128, 1024], BF16, OFF["silu"])
        mk("wdT21", [128, 1024], BF16, OFF["xin1"])
        mk("i32h", [128, 512], I32, OFF["xin1"] + 2048)
        mk("ident_bf", [128, 128], BF16, OFF["ident_bf"])
        mk("onecb", [128, 1], BF16, OFF["onecb"])
        mk("rdenb", [128, 512], F32, OFF["kftm"])
        mk("ident_f32", [128, 128], F32, OFF["ident_f32"])
        mk("onesf", [1, 128], F32, OFF["onesf"])
        mk("onesb", [1, 128], BF16, OFF["onesb"])
        for n, w in (("A_st", 16), ("B_st", 16), ("AQ", 16), ("BQ", 16),
                     ("lnmax", 16), ("habs", 88), ("wsg", 88), ("stat6", 12),
                     ("mv", 2), ("red1", 1), ("red2", 1),
                     ("bq_sb", 8), ("bk_sb", 8), ("ksum", 8)):
            mk(n, [128, w], F32, OFF[n])
        mk("bcol", [128, 84], F32, OFF["bcol"])
        mk("wsga", [128, 64], F32, OFF["wsga"])
        mk("srow", [1, 64], F32, OFF["srow"])
        mk("grow", [1, 8], F32, OFF["grow"])
        mk("pT_f32", [64, 64], F32, OFF["pT_f32"])
        mk("pnT", [128, 64], BF16, OFF["pnT"])
        mk("bvrow", [1, 1024], BF16, OFF["bvrow"])
        mk("borow", [1, 1024], BF16, OFF["borow"])
        mk("clip_sb", [1, 2], F32, OFF["clip_sb"])
        mk("bd", [128, 1024], BF16, OFF["bd"])
        mk("kv_sb", [128, 512], BF16, OFF["kv_sb"])
        mk("xin0", [128, 1024], F32, OFF["xin0"])
        mk("xin1", [128, 1024], F32, OFF["xin1"])
        mk("cst", [128, 8], F32, OFF["cst"])
        mk("dbgw", [128, 1384], F32, OFF["silu"])

    with (
        nc.sbuf_tensor("arena", [128, AB], U8) as arena_t,
        nc.psum_tensor("pm0", [128, 512], F32) as pm0,
        nc.psum_tensor("pm1", [128, 512], F32) as pm1,
        nc.psum_tensor("pm2", [128, 512], F32) as pm2,
        nc.psum_tensor("pm3", [128, 512], F32) as pm3,
        nc.psum_tensor("ptA", [128, 1024], BF16) as ptA,
        nc.psum_tensor("ptB", [128, 1024], BF16) as ptB,
        nc.psum_tensor("pkv", [128, 512], F32) as pkv,
        nc.psum_tensor("paux", [128, 512], F32) as paux,
        nc.semaphore("PE") as s2,
        nc.semaphore("VE") as s3,
        nc.semaphore("SE") as s4,
        nc.semaphore("GE") as s5,
        nc.semaphore("CC") as s6,
    ):
        import contextlib as _ctx
        _stack = _ctx.ExitStack()
        dma_sems = {}
        for qn in ("SXD", "GDD"):
            for q in range(DMA_POOL):
                dma_sems[f"{qn}{q}"] = _stack.enter_context(
                    nc.semaphore(f"{qn}{q}"))
        mkviews(nc.lookup_mloc(arena_t).addr)
        nc.const_aps.register(F32, 0.0, V["cst"][:, 0:1])
        nc.const_aps.register(F32, 1.0, V["cst"][:, 1:2])
        nc.const_aps.register(F32, EPS_AT, V["cst"][:, 2:3])
        sems = {"PE": s2, "VE": s3, "SE": s4, "GE": s5, "CC": s6}
        sems.update(dma_sems)
        _dctr = {"vector": 0, "scalar": 0}

        def _vdummy(e):
            c = 4 + (_dctr["vector"] % 4)
            _dctr["vector"] += 1
            return e.tensor_copy(V["cst"][:, c:c + 1], V["cst"][:, 1:2])

        def _sdummy(e):
            return e.copy(V["cst"][:, 3:4], V["cst"][:, 1:2])

        kb = KB(dummy_builders={"vector": _vdummy, "scalar": _sdummy})
        emit_all(kb, V, H, [pm0, pm1, pm2, pm3], ptA, ptB, pkv, paux)
        kb.resolve()
        with nc.Block() as block:
            @block.sync
            def _(e):
                kb.replay("sync", e, sems)

            @block.vector
            def _(e):
                kb.replay("vector", e, sems)

            @block.scalar
            def _(e):
                kb.replay("scalar", e, sems)

            @block.tensor
            def _(e):
                kb.replay("tensor", e, sems)

            @block.gpsimd
            def _(e):
                kb.replay("gpsimd", e, sems)
        _stack.close()
    return nc



def emit_all(kb, V, H, pm, ptA, ptB, pkv, paux):
    E = kb.emit

    def xres_t(tc):
        return V["x_res"][:, 1024 * tc:1024 * (tc + 1)]

    def fmA(c):
        return V["actA"][:, 2048 * c:2048 * (c + 1)]

    def h_c(j):
        return V["harena"][:, 2048 * j:2048 * (j + 1)]

    def wdT(j):
        if j < 16:
            return V["actA"][:, 1024 * j:1024 * (j + 1)]
        return V[f"wdT{j}"][:, :]

    def wdT_name(j):
        return f"wdT:{j}"

    def wAc(k):
        return V["wA"][:, 1024 * k:1024 * (k + 1)]

    def wBc(k):
        return V["wB"][:, 1024 * k:1024 * (k + 1)]

    E("gpsimd", lambda e: e.memset(V["cst"][:, 0:1], 0.0), writes=["cst"])
    E("gpsimd", lambda e: e.memset(V["cst"][:, 1:2], 1.0), writes=["cst"])
    kb.barrier()
    E("gpsimd", lambda e: e.memset(V["ident_bf"][:, :], 0.0), writes=["identb"])
    E("gpsimd", lambda e: e.affine_select(
        out=V["ident_bf"][:, :], in_=V["ident_bf"][:, :], compare_op=ALU.not_equal,
        fill=1.0, base=0, pattern=[[-1, 128]], channel_multiplier=1),
        reads=["identb"], writes=["identb"])
    E("gpsimd", lambda e: e.memset(V["ident_f32"][:, :], 0.0), writes=["identf"])
    E("gpsimd", lambda e: e.affine_select(
        out=V["ident_f32"][:, :], in_=V["ident_f32"][:, :], compare_op=ALU.not_equal,
        fill=1.0, base=0, pattern=[[-1, 128]], channel_multiplier=1),
        reads=["identf"], writes=["identf"])
    E("gpsimd", lambda e: e.memset(V["onesf"][:, :], 1.0), writes=["onesf"])
    E("gpsimd", lambda e: e.memset(V["onesb"][:, :], 1.0), writes=["onesb"])
    E("gpsimd", lambda e: e.memset(V["onecb"][:, :], 1.0), writes=["onecb"])
    E("gpsimd", lambda e: e.memset(V["cst"][:, 2:3], EPS_AT), writes=["cst"])

    E("sync", lambda e: e.dma_start(out=V["clip_sb"][:, 0:1], in_=H["clip_x"]),
      kind="d", writes=["clip"])
    E("sync", lambda e: e.dma_start(out=V["clip_sb"][:, 1:2], in_=H["clip_down"]),
      kind="d", writes=["clip2"])
    E("sync", lambda e: e.dma_start(out=V["bq_sb"][:, :], in_=H["bq"]),
      kind="d", writes=["bq"])
    E("sync", lambda e: e.dma_start(out=V["bk_sb"][:, :], in_=H["bk"]),
      kind="d", writes=["bk"])
    E("sync", lambda e: e.dma_start(out=V["pT_f32"][:, :], in_=H["pT"]),
      kind="d", writes=["pT"])
    for tc in range(NT):
        E("sync", lambda e, t=tc: e.dma_start(
            out=xres_t(t), in_=H["x_in"][128 * t:128 * (t + 1), :]),
            kind="d", writes=[f"x:{tc}"])
    E("vector", lambda e: e.tensor_scalar_mul(V["pnT"][0:64, :], V["pT_f32"][:, :], NORM),
      reads=["pT"], writes=["pnT"])
    E("vector", lambda e: e.tensor_scalar_mul(V["pnT"][64:128, :], V["pT_f32"][:, :], NORM),
      reads=["pT"], writes=["pnT"])

    def ln_stats(tc, xap, xname, absmax=False):
        E("vector", lambda e, x=xap: e.bn_stats(V["stat6"][:, 0:6], x[:, 0:512]),
          reads=[xname], writes=["stat6"], late=True)
        E("vector", lambda e, x=xap: e.bn_stats(V["stat6"][:, 6:12], x[:, 512:1024]),
          reads=[xname], writes=["stat6"], late=True)
        E("vector", lambda e: e.bn_aggr(V["mv"][:, :], V["stat6"][:, :]),
          reads=["stat6"], writes=["mv"], late=True)
        E("vector", lambda e: e.tensor_scalar_add(V["red1"][:, :], V["mv"][:, 1:2], EPS_LN),
          reads=["mv"], writes=["red1"])
        E("scalar", lambda e: e.activation(V["red2"][:, :], V["red1"][:, :], AF.Sqrt),
          reads=["red1"], writes=["red2"])
        E("vector", lambda e, t=tc: e.reciprocal(V["A_st"][:, t:t + 1], V["red2"][:, :]),
          reads=["red2"], writes=["A_st"], late=True)
        E("vector", lambda e, t=tc: e.scalar_tensor_tensor(
            V["B_st"][:, t:t + 1], V["mv"][:, 0:1], -1.0, V["A_st"][:, t:t + 1],
            ALU.mult, ALU.mult), reads=["mv", "A_st"], writes=["B_st"])
        if absmax:
            E("scalar", lambda e, x=xap, t=tc: e.activation(
                V["qscr"][:, :], x, AF.Abs,
                bias=V["B_st"][:, t:t + 1], scale=V["A_st"][:, t:t + 1]),
                reads=[xname, "A_st", "B_st"], writes=["qscr"])
            E("vector", lambda e, t=tc: e.reduce_max(
                V["lnmax"][:, t:t + 1], V["qscr"][:, :], axis=AX.X),
                reads=["qscr"], writes=["lnmax"], late=True)

    def partition_max(col_ap, colname, dst11, dstname):
        E("tensor", lambda e, c=col_ap: e.transpose(
            paux[0:1, 0:128], c, V["ident_f32"][:, :]),
            reads=[colname, "identf"], writes=["paux"])
        E("vector", lambda e, d=dst11: e.reduce_max(d, paux[0:1, 0:128], axis=AX.X),
          reads=["paux"], writes=[dstname], late=True)

    def global_max_launch(idx, src11, srcname):
        E("gpsimd", lambda e, i=idx, s=src11: e.dma_start(out=H["cc_in"][i], in_=s),
          kind="d", reads=[srcname], writes=[f"ccin{idx}"])
        E("gpsimd", lambda e, i=idx: e.collective_compute(
            "AllGather", ALU.bypass, ins=[H["cc_in"][i].opt()],
            outs=[H["cc_out"][i].opt()], replica_groups=[list(range(8))]),
            kind="cc", reads=[f"ccin{idx}"], writes=[f"ccout{idx}"])
        E("gpsimd", lambda e, i=idx: e.dma_start(out=V["grow"][:, :], in_=H["cc_out"][i]),
          kind="d", reads=[f"ccout{idx}"], writes=["grow"])

    def global_max_finish(dst11, dstname):
        E("vector", lambda e, d=dst11: e.reduce_max(d, V["grow"][:, :], axis=AX.X),
          reads=["grow"], writes=[dstname], late=True)

    def global_max(idx, src11, srcname, dst11, dstname):
        global_max_launch(idx, src11, srcname)
        global_max_finish(dst11, dstname)

    def bcast(dst_col, dcolname, src11, srcname):
        E("tensor", lambda e, x=src11: e.matmul(
            paux[:, 4:5], V["onesf"][:, :], x, start=True, stop=True),
            reads=[srcname, "onesf"], writes=["paux"])
        E("vector", lambda e, d=dst_col: e.tensor_copy(d, paux[:, 4:5]),
          reads=["paux"], writes=[dcolname])

    def sr(i):
        return V["srow"][:, i:i + 1]

    def bc(i):
        return V["bcol"][:, i:i + 1]

    def scale_chain(gi, sig_i, s_i, inv_i, bc_inv, bc_s, half=False, half_i=None,
                    bc_half=None):
        E("vector", lambda e: e.tensor_tensor(sr(s_i), sr(gi), sr(sig_i), ALU.mult),
          reads=["srow"], writes=["srow"])
        E("vector", lambda e: e.tensor_scalar_mul(sr(s_i), sr(s_i), 1.0 / QMAX),
          reads=["srow"], writes=["srow"])
        E("vector", lambda e: e.reciprocal(sr(inv_i), sr(s_i)),
          reads=["srow"], writes=["srow"], late=True)
        bcast(bc(bc_inv), "bcol", sr(inv_i), "srow")
        bcast(bc(bc_s), "bcol", sr(s_i), "srow")
        if half:
            E("vector", lambda e: e.tensor_scalar_mul(sr(half_i), sr(s_i), 0.5),
              reads=["srow"], writes=["srow"])
            bcast(bc(bc_half), "bcol", sr(half_i), "srow")

    def quant_transpose(tc, xap, xname, guard=None, guard_reads=None):
        E("scalar", lambda e, x=xap, t=tc: e.activation(
            V["qscr"][:, :], x, AF.Identity,
            bias=V["BQ"][:, t:t + 1], scale=V["AQ"][:, t:t + 1]),
            reads=[xname, "AQ", "BQ"], writes=["qscr"])
        E("vector", lambda e: e.tensor_scalar(
            V["i32"][:, :], V["qscr"][:, :], 127.49, -128.49, ALU.min, ALU.max),
            reads=["qscr"], writes=["i32s"])
        E("vector", lambda e: e.tensor_copy(V["astage"][:, :], V["i32"][:, :]),
          reads=["i32s"], writes=["astage"])
        for c in range(DC):
            pt, ptn = (ptA, "ptA") if c % 2 == 0 else (ptB, "ptB")
            E("tensor", lambda e, c_=c, p=pt: e.transpose(
                p[:, 0:128], V["astage"][:, 128 * c_:128 * (c_ + 1)], V["ident_bf"][:, :]),
                reads=["astage", "identb"], writes=[ptn])
            rd = [ptn]
            if guard_reads and tc == 0 and c == 0:
                rd = rd + guard_reads
            E("scalar", lambda e, c_=c, t=tc, p=pt: e.activation(
                fmA(c_)[:, 128 * t:128 * (t + 1)], p[:, 0:128], AF.Identity),
                reads=rd, writes=[f"fmA:{c}"],
                extra=(guard if (guard and tc == 0 and c == 0) else ()))

    def emit_coop_prep():
        E("vector", lambda e: e.memset(V["wsg"][:, 6:8], 0.0), writes=["wsg"])
        for s in range(6):
            i, gu = s // 2, s % 2
            src = H["wg_mine"] if gu == 0 else H["wu_mine"]
            nat, natn = ((V["pnat0"], "pnat0") if s % 2 == 0
                         else (V["pnat1"], "pnat1"))
            stg, stgn = ((V["pstg0"], "pstg0") if s % 2 == 0
                         else (V["pstg1"], "pstg1"))
            E("sync", lambda e, i_=i, sc=src, o=nat: e.dma_start(
                out=o[:, :], in_=sc[128 * i_:128 * (i_ + 1), :]),
                kind="d", writes=[natn])
            E("vector", lambda e, n=nat: e.reduce_max(
                V["red1"][:, :], n[:, :], axis=AX.X, apply_absolute_value=True),
                reads=[natn], writes=["red1"], late=True)
            E("vector", lambda e: e.reciprocal(V["red2"][:, :], V["red1"][:, :]),
              reads=["red1"], writes=["red2"], late=True)
            E("vector", lambda e: e.tensor_scalar_mul(
                V["red2"][:, :], V["red2"][:, :], QMAX),
                reads=["red2"], writes=["red2"])
            E("vector", lambda e, n=nat: e.tensor_scalar(
                V["i32"][:, :], n[:, :], V["red2"][:, :], 127.49,
                ALU.mult, ALU.min), reads=[natn, "red2"], writes=["i32s"])
            E("vector", lambda e: e.tensor_scalar(
                V["pq"][:, :], V["i32"][:, :], -128.0, None, ALU.max),
                reads=["i32s"], writes=["pq"])
            E("vector", lambda e, s_=s: e.tensor_scalar_mul(
                V["wsg"][:, s_:s_ + 1], V["red1"][:, :], 1.0 / QMAX),
                reads=["red1"], writes=["wsg"])
            for k in range(DC):
                pt, ptn = (ptA, "ptA") if k % 2 == 0 else (ptB, "ptB")
                E("tensor", lambda e, k_=k, p=pt: e.transpose(
                    p[:, 0:128], V["pq"][:, 128 * k_:128 * (k_ + 1)],
                    V["ident_bf"][:, :]), reads=["pq", "identb"], writes=[ptn])
                E("scalar", lambda e, k_=k, p=pt, o=stg: e.activation(
                    o[:, 128 * k_:128 * (k_ + 1)], p[:, 0:128], AF.Identity),
                    reads=[ptn], writes=[stgn])
            E("gpsimd", lambda e, i_=i, g_=gu, o=stg: e.dma_start(
                out=H[f"myg{i_}"][128 * g_:128 * (g_ + 1), :], in_=o[:, :]),
                kind="d", reads=[stgn], writes=[f"myg:{s}"])
            if gu == 1:
                if i == 2:
                    E("gpsimd", lambda e: e.dma_start(out=H["mysc"],
                      in_=V["wsg"][:, 0:8]),
                      kind="d", reads=["wsg"], writes=["mysc"])
                    E("gpsimd", lambda e: e.collective_compute(
                        "AllGather", ALU.bypass, ins=[H["mysc"].opt()],
                        outs=[H["allsc"].opt()],
                        replica_groups=[list(range(8))]), kind="cc",
                        reads=["mysc"], writes=["ccoutsc"])
                E("gpsimd", lambda e, i_=i: e.collective_compute(
                    "AllGather", ALU.bypass, ins=[H[f"myg{i_}"].opt()],
                    outs=[H[f"allg{i_}"].opt()],
                    replica_groups=[list(range(8))]), kind="cc",
                    reads=[f"myg:{2 * i}", f"myg:{2 * i + 1}"],
                    writes=[f"ccoutg{i}"])
        gu_pe = [kb.mark("tensor")]
        E("sync", lambda e: e.dma_start(out=V["pwdn"][:, :], in_=H["wd_mine"]),
          kind="d", writes=["pwdn"])
        E("vector", lambda e: e.reduce_max(
            V["red1"][:, :], V["pwdn"][:, :], axis=AX.X,
            apply_absolute_value=True), reads=["pwdn"], writes=["red1"], late=True)
        E("vector", lambda e: e.reciprocal(V["red2"][:, :], V["red1"][:, :]),
          reads=["red1"], writes=["red2"], late=True)
        E("vector", lambda e: e.tensor_scalar_mul(V["red2"][:, :], V["red2"][:, :], QMAX),
          reads=["red2"], writes=["red2"])
        E("vector", lambda e: e.tensor_scalar_mul(V["red1"][:, :], V["red1"][:, :],
                                                  1.0 / QMAX),
          reads=["red1"], writes=["red1"])
        for hf in range(3):
            a, b = 1024 * hf, min(II, 1024 * (hf + 1))
            E("vector", lambda e, a_=a, b_=b: e.tensor_scalar(
                V["i32"][:, 0:b_ - a_], V["pwdn"][:, a_:b_], V["red2"][:, :], 127.49,
                ALU.mult, ALU.min),
                reads=["pwdn", "red2", "myg:4", "myg:5", "i32s"], writes=["i32s"],
                extra=(gu_pe if hf == 0 else ()))
            E("vector", lambda e, a_=a, b_=b: e.tensor_scalar(
                V["pwdq"][:, a_:b_], V["i32"][:, 0:b_ - a_], -128.0, V["red1"][:, :],
                ALU.max, ALU.mult), reads=["i32s", "red1"], writes=["pwdq"])
        gu_ve = [kb.mark("vector")]
        for j in range(JC):
            pt, ptn = (ptA, "ptA") if j % 2 == 0 else (ptB, "ptB")
            E("tensor", lambda e, j_=j, p=pt: e.transpose(
                p[:, 0:128], V["pwdq"][:, 128 * j_:128 * (j_ + 1)],
                V["ident_bf"][:, :]), reads=["pwdq", "identb"], writes=[ptn])
            E("scalar", lambda e, j_=j, p=pt: e.activation(
                V["pwdt"][:, 128 * j_:128 * (j_ + 1)], p[:, 0:128], AF.Identity),
                reads=[ptn], writes=["pwdt"], extra=gu_ve)
            E("gpsimd", lambda e, j_=j: e.dma_start(
                out=H["mywd"][128 * j_:128 * (j_ + 1), :],
                in_=V["pwdt"][:, 128 * j_:128 * (j_ + 1)]),
                kind="d", reads=["pwdt"], writes=[f"mywd:{j}"])
        E("gpsimd", lambda e: e.collective_compute(
            "AllGather", ALU.bypass, ins=[H["mywd"].opt()], outs=[H["allwd"].opt()],
            replica_groups=[list(range(8))]), kind="cc",
            reads=[f"mywd:{j}" for j in range(JC)], writes=["ccoutwd"])
        for c in range(8):
            E("sync", lambda e, c_=c: e.dma_start(
                out=V["wsga"][:, 8 * c_:8 * (c_ + 1)],
                in_=H["allsc"][128 * c_:128 * (c_ + 1), :]),
                kind="d", reads=["ccoutsc"], writes=["wsga"])
        return [kb.mark("tensor"), kb.mark("scalar")]

    def emit_wd_repack():
        for j in range(JC):
            for m in range(DC):
                E("gpsimd", lambda e, j_=j, m_=m: e.dma_start(
                    out=H["wdT_h"][128 * j_:128 * (j_ + 1), 128 * m_:128 * (m_ + 1)],
                    in_=H["allwd"][II * m_ + 128 * j_:II * m_ + 128 * (j_ + 1), :]),
                    kind="d", reads=["ccoutwd"], writes=[f"wdTh:{j}:{m}"])

    def emit_ffn(src_resident, src_hbm, dst_store, sig_idx, sgi, tag,
                 src_names=None, post_tile_hook=None, skip_stats=False,
                 prelaunched=False, quant_guard=None, quant_guard_reads=None):
        if not prelaunched:
            if not skip_stats:
                for tc in range(NT):
                    ln_stats(tc, xres_t(tc), f"x:{tc}", absmax=True)
            E("vector", lambda e: e.reduce_max(V["red2"][:, :], V["lnmax"][:, :],
                                               axis=AX.X),
              reads=["lnmax"], writes=["red2"], late=True)
            partition_max(V["red2"][:, :], "red2", sr(sgi), "srow")
            global_max_launch(0 if tag == "ffn1" else 2, sr(sgi), "srow")
        global_max_finish(sr(sgi + 1), "srow")
        scale_chain(sgi + 1, sig_idx, sgi + 2, sgi + 3, bc_inv=0, bc_s=1)
        E("vector", lambda e: e.tensor_scalar(
            V["AQ"][:, :], V["A_st"][:, :], bc(0), None, ALU.mult),
            reads=["A_st", "bcol"], writes=["AQ"])
        E("vector", lambda e: e.tensor_scalar(
            V["BQ"][:, :], V["B_st"][:, :], bc(0), None, ALU.mult),
            reads=["B_st", "bcol"], writes=["BQ"])
        for tc in range(NT):
            quant_transpose(tc, xres_t(tc), f"x:{tc}",
                            guard=quant_guard, guard_reads=quant_guard_reads)
        if os.environ.get("DBGF") and tag == "ffn1":
            E("gpsimd", lambda e: e.dma_start(out=H["out"][2036:2037, 0:128],
              in_=V["onesb"][:, :]), kind="d", reads=["onesb"], writes=["df0"])
        kb.barrier()

        E("vector", lambda e: e.tensor_scalar(
            V["bcol"][:, 16:80], V["wsga"][:, :], bc(1), None, ALU.mult),
            reads=["wsga", "bcol"], writes=["sgu"])

        def sgu_c(jc, gu):
            return 16 + 8 * (jc % 8) + 2 * (jc // 8) + gu

        if tag == "ffn1":
            emit_wd_repack()
        for j in (16, 17, 18, 19, 21):
            E("gpsimd", lambda e, j_=j: e.dma_start(
                out=wdT(j_)[:, :], in_=H["wdT_h"][128 * j_:128 * (j_ + 1), :]),
                kind="d", reads=[f"wdTh:{j}:{m}" for m in range(DC)],
                writes=[f"wdT:{j}"])

        for jc in range(JC):
            nb = jc % 2
            wgT, wuT = V[f"wgT{nb}"], V[f"wuT{nb}"]
            si, gr = jc // 8, 256 * (jc % 8)
            E("sync", lambda e, o=wgT, i_=si, r=gr: e.dma_start(
                out=o[:, :], in_=H[f"allg{i_}"][r:r + 128, :]),
                kind="d", reads=[f"ccoutg{si}"], writes=[f"wgT{nb}"])
            E("sync", lambda e, o=wuT, i_=si, r=gr: e.dma_start(
                out=o[:, :], in_=H[f"allg{i_}"][r + 128:r + 256, :]),
                kind="d", reads=[f"ccoutg{si}"], writes=[f"wuT{nb}"])
            for tt in range(NTT):
                bg, bgn = pm[(2 * tt) % 4], f"pm{(2 * tt) % 4}"
                bu, bun = pm[(2 * tt + 1) % 4], f"pm{(2 * tt + 1) % 4}"
                sl, sln = ((V["silu"], "silu") if tt % 2 == 0
                           else (V["silu2"], "silu2"))
                for kc in range(DC):
                    E("tensor", lambda e, k=kc, t=tt, b=bg, w=wgT: e.matmul(
                        b[:, :], w[:, 128 * k:128 * (k + 1)],
                        fmA(k)[:, 512 * t:512 * (t + 1)],
                        start=(k == 0), stop=(k == DC - 1)),
                        reads=[f"wgT{nb}", f"fmA:{kc}"], writes=[bgn])
                for kc in range(DC):
                    E("tensor", lambda e, k=kc, t=tt, b=bu, w=wuT: e.matmul(
                        b[:, :], w[:, 128 * k:128 * (k + 1)],
                        fmA(k)[:, 512 * t:512 * (t + 1)],
                        start=(k == 0), stop=(k == DC - 1)),
                        reads=[f"wuT{nb}", f"fmA:{kc}"], writes=[bun])
                cg, cu = sgu_c(jc, 0), sgu_c(jc, 1)
                E("scalar", lambda e, b=bg, c=cg, s_=sl: e.activation(
                    s_[:, :], b[:, :], AF.Silu,
                    scale=V["bcol"][:, c:c + 1]),
                    reads=[bgn, "sgu"], writes=[sln])
                E("vector", lambda e, b=bu, c=cu, j=jc, t=tt, s_=sl: e.scalar_tensor_tensor(
                    h_c(j)[:, 512 * t:512 * (t + 1)], b[:, :],
                    V["bcol"][:, c:c + 1], s_[:, :],
                    ALU.mult, ALU.mult),
                    reads=[bun, "sgu", sln], writes=[f"h:{jc}"])
                E("vector", lambda e, j=jc, t=tt: e.reduce_max(
                    V["habs"][:, 4 * j + t:4 * j + t + 1],
                    h_c(j)[:, 512 * t:512 * (t + 1)], axis=AX.X,
                    apply_absolute_value=True),
                    reads=[f"h:{jc}"], writes=["habs"], late=True)

        if os.environ.get("DBGF") and tag == "ffn1":
            E("gpsimd", lambda e: e.dma_start(out=H["out"][2037:2038, 0:128],
              in_=V["onesb"][:, :]), kind="d", reads=["onesb"], writes=["df1"])
        E("vector", lambda e: e.reduce_max(V["red2"][:, :], V["habs"][:, 0:88], axis=AX.X),
          reads=["habs"], writes=["red2"], late=True)
        partition_max(V["red2"][:, :], "red2", sr(sgi + 8), "srow")
        global_max_launch(1 if tag == "ffn1" else 3, sr(sgi + 8), "srow")

        alias_deps = [kb.mark("tensor"), kb.mark("vector"), kb.mark("scalar")]
        for qi, j in enumerate(j for j in range(JC) if j not in (16, 17, 18, 19, 21)):
            eng = "sync" if qi % 2 == 0 else "gpsimd"
            E(eng, lambda e, j_=j: e.dma_start(
                out=wdT(j_)[:, :], in_=H["wdT_h"][128 * j_:128 * (j_ + 1), :]),
                kind="d", reads=[f"wdTh:{j}:{m}" for m in range(DC)],
                writes=[f"wdT:{j}"], extra=alias_deps)

        global_max_finish(sr(sgi + 9), "srow")
        E("vector", lambda e: e.tensor_tensor(sr(63), sr(sgi + 9), sr(34),
                                              ALU.mult), reads=["srow"], writes=["srow"])
        E("vector", lambda e: e.tensor_scalar_mul(sr(sgi + 10), sr(63), 1.0 / QMAX),
          reads=["srow"], writes=["srow"])
        E("vector", lambda e: e.reciprocal(sr(sgi + 11), sr(sgi + 10)),
          reads=["srow"], writes=["srow"], late=True)
        E("vector", lambda e: e.tensor_scalar_mul(sr(sgi + 12), sr(sgi + 10), 0.5),
          reads=["srow"], writes=["srow"])
        bcast(bc(2), "bcol", sr(sgi + 11), "srow")
        bcast(bc(3), "bcol", sr(sgi + 12), "srow")

        kb.barrier()
        def quant_block(b, part=None):
            lo, hi = part if part else (0, JC)
            for jc in range(lo, hi):
                E("vector", lambda e, j=jc, b_=b: e.tensor_scalar(
                    V["i32h"][:, :], h_c(j)[:, 512 * b_:512 * (b_ + 1)],
                    bc(2), 127.49, ALU.mult, ALU.min),
                    reads=[f"h:{jc}", "bcol"], writes=["i32hs"])
                E("vector", lambda e, j=jc, b_=b: e.tensor_scalar(
                    h_c(j)[:, 512 * b_:512 * (b_ + 1)], V["i32h"][:, :],
                    -128.0, None, ALU.max),
                    reads=["i32hs"], writes=[f"h:{jc}"])

        quant_block(0)
        for tc in range(NT):
            xin, xinn = V["xin0"], "xin0"
            E("sync", lambda e, t=tc, o=xin: e.dma_start(
                out=o[:, :], in_=src_hbm[128 * t:128 * (t + 1), :]),
                kind="d", writes=[xinn],
                reads=[src_names.format(tc)] if src_names else [])
            for mb in range(2):
                b, bn = pm[2 * (tc % 2) + mb], f"pm{2 * (tc % 2) + mb}"
                for jc in range(JC):
                    E("tensor", lambda e, j=jc, t=tc, m=mb, b_=b: e.matmul(
                        b_[:, :], h_c(j)[:, 128 * t:128 * (t + 1)],
                        wdT(j)[:, 512 * m:512 * (m + 1)],
                        start=(j == 0), stop=(j == JC - 1)),
                        reads=[f"h:{jc}", f"wdT:{jc}"], writes=[bn])
                E("vector", lambda e, t=tc, m=mb, b_=b, xi=xin: e.scalar_tensor_tensor(
                    xres_t(t)[:, 512 * m:512 * (m + 1)], b_[:, :], bc(3),
                    xi[:, 512 * m:512 * (m + 1)], ALU.mult, ALU.add),
                    reads=[bn, "bcol", xinn], writes=[f"x:{tc}"])
            b_next = tc // 4 + 1
            if 4 * b_next < NT:
                qr = ((0, 6), (6, 12), (12, 18), (18, 22))[tc % 4]
                quant_block(b_next, qr)
            if dst_store is not None:
                E("gpsimd", lambda e, t=tc: e.dma_start(
                    out=dst_store[128 * t:128 * (t + 1), :], in_=xres_t(t)),
                    kind="d", reads=[f"x:{tc}"], writes=[f"xs:{tc}"])
            if post_tile_hook is not None:
                post_tile_hook(tc)
        kb.barrier()

    def emit_attn(dbg=None, skip_stats=False):
        E("sync", lambda e: e.dma_start(out=V["qscr"][0:1, :], in_=H["bvr"]),
          kind="d", writes=["qscr"])
        E("vector", lambda e: e.tensor_copy(V["bvrow"][:, :], V["qscr"][0:1, :]),
          reads=["qscr"], writes=["bvrow"])
        E("sync", lambda e: e.dma_start(out=V["qscr"][0:1, :], in_=H["bor"]),
          kind="d", writes=["qscr"])
        E("vector", lambda e: e.tensor_copy(V["borow"][:, :], V["qscr"][0:1, :]),
          reads=["qscr"], writes=["borow"])
        if dbg is not None:
            E("gpsimd", lambda e: e.dma_start(out=dbg[2039:2040, 0:128],
              in_=V["onesb"][:, :]), kind="d", reads=["onesb"], writes=["do0"])
            E("gpsimd", lambda e: e.dma_start(out=dbg[2030:2031, 0:1024],
              in_=V["dbgw"][0:1, 360:1384]), kind="d", reads=["onesb"], writes=["dw0"])
            E("gpsimd", lambda e: e.dma_start(out=dbg[2031:2032, 0:1024],
              in_=V["dbgw"][64:65, 360:1384]), kind="d", reads=["onesb"], writes=["dw1"])
            E("gpsimd", lambda e: e.dma_start(out=dbg[2032:2033, 0:1024],
              in_=V["dbgw"][1:2, 360:1384]), kind="d", reads=["onesb"], writes=["dw2"])
        for tc in range(NT):
            if not skip_stats:
                ln_stats(tc, xres_t(tc), f"x:{tc}")
            E("scalar", lambda e, t=tc: e.activation(
                V["qscr"][:, :], xres_t(t), AF.Identity,
                bias=V["B_st"][:, t:t + 1], scale=V["A_st"][:, t:t + 1]),
                reads=[f"x:{tc}", "A_st", "B_st"], writes=["qscr"])
            E("vector", lambda e: e.tensor_copy(V["astage"][:, :], V["qscr"][:, :]),
              reads=["qscr"], writes=["astage"])
            for c in range(DC):
                pt, ptn = (ptA, "ptA") if c % 2 == 0 else (ptB, "ptB")
                E("tensor", lambda e, c_=c, p=pt: e.transpose(
                    p[:, 0:128], V["astage"][:, 128 * c_:128 * (c_ + 1)],
                    V["ident_bf"][:, :]), reads=["astage", "identb"], writes=[ptn])
                E("vector", lambda e, c_=c, t=tc, p=pt: e.tensor_copy(
                    fmA(c_)[:, 128 * t:128 * (t + 1)], p[:, 0:128]),
                    reads=[ptn], writes=[f"fmA:{c}"])
        kb.barrier()

        def wprep(w_hbm, dstf, dsttag):
            for mc in range(DC):
                nat, natn = ((V["wnat_a"], "wnat_a") if mc % 2 == 0
                             else (V["wnat_b"], "wnat_b"))
                stg, stgn = ((V["astage"], "astage") if mc % 2 == 0
                             else (V["astage2"], "astage2"))
                E("sync", lambda e, m=mc, o=nat: e.dma_start(
                    out=o[:, :], in_=w_hbm[128 * m:128 * (m + 1), :]),
                    kind="d", writes=[natn])
                E("scalar", lambda e, n=nat, o=stg: e.activation(
                    o[:, :], n[:, :], AF.Identity),
                    reads=[natn], writes=[stgn])
                for kcb in range(DC):
                    pt, ptn = (ptA, "ptA") if kcb % 2 == 0 else (ptB, "ptB")
                    E("tensor", lambda e, k=kcb, p=pt, s_=stg: e.transpose(
                        p[:, 0:128], s_[:, 128 * k:128 * (k + 1)],
                        V["ident_bf"][:, :]), reads=[stgn, "identb"], writes=[ptn])
                    if kcb % 2 == 0:
                        E("scalar", lambda e, k=kcb, m=mc, p=pt, d=dstf: e.activation(
                            d(k)[:, 128 * m:128 * (m + 1)], p[:, 0:128], AF.Identity),
                            reads=[ptn], writes=[f"{dsttag}:{kcb}"])
                    else:
                        E("vector", lambda e, k=kcb, m=mc, p=pt, d=dstf: e.tensor_copy(
                            d(k)[:, 128 * m:128 * (m + 1)], p[:, 0:128]),
                            reads=[ptn], writes=[f"{dsttag}:{kcb}"])

        if dbg is not None:
            E("gpsimd", lambda e: e.dma_start(out=dbg[2040:2041, 0:128],
              in_=V["onesb"][:, :]), kind="d", reads=["onesb"], writes=["do1"])
            E("gpsimd", lambda e: e.dma_start(out=dbg[2043:2044, 0:128],
              in_=V["onesf"][:, :]), kind="d", reads=["onesf"], writes=["do4"])
            E("gpsimd", lambda e: e.dma_start(out=dbg[2044:2045, 0:16],
              in_=V["A_st"][0:1, :]), kind="d", reads=["A_st"], writes=["do5"])
            E("gpsimd", lambda e: e.dma_start(out=dbg[2044:2045, 16:32],
              in_=V["B_st"][0:1, :]), kind="d", reads=["B_st"], writes=["do6"])
            E("gpsimd", lambda e: e.dma_start(out=dbg[2044:2045, 32:64],
              in_=V["ident_f32"][0:1, 0:32]), kind="d", reads=["identf"], writes=["do7"])
        wprep(H["wk"], wAc, "wA")
        if dbg is not None:
            E("gpsimd", lambda e: e.dma_start(out=dbg[2041:2042, 0:128],
              in_=V["onesb"][:, :]), kind="d", reads=["onesb"], writes=["do2"])
        wprep(H["wv"], wBc, "wB")
        if dbg is not None:
            E("gpsimd", lambda e: e.dma_start(out=dbg[2042:2043, 0:128],
              in_=V["onesb"][:, :]), kind="d", reads=["onesb"], writes=["do3"])

        SAFE = bool(os.environ.get("SAFEP1"))
        for tt in range(NTT):
            do_dump = (dbg is not None and tt == 0 and not os.environ.get("DBGP2"))
            if SAFE:
                kb.barrier()
            for mc in range(DC):
                b, bn = pm[mc % 2], f"pm{mc % 2}"
                for kc in range(DC):
                    E("tensor", lambda e, k=kc, m=mc, t=tt, b_=b: e.matmul(
                        b_[:, :], wAc(k)[:, 128 * m:128 * (m + 1)],
                        fmA(k)[:, 512 * t:512 * (t + 1)],
                        start=(k == 0), stop=(k == DC - 1)),
                        reads=[f"wA:{kc}", f"fmA:{kc}"], writes=[bn])
                E("scalar", lambda e, m=mc, b_=b: e.activation(
                    V["k_scr"][:, 512 * m:512 * (m + 1)], b_[:, :], AF.Identity,
                    bias=V["bk_sb"][:, m:m + 1]),
                    reads=[bn, "bk"], writes=[f"kscr:{mc}"])
            for c in range(DC):
                b, bn = pm[2 + c % 2], f"pm{2 + c % 2}"
                E("tensor", lambda e, c_=c, b_=b: e.matmul(
                    b_[0:64, :], V["pnT"][0:64, :],
                    V["k_scr"][0:64, 512 * c_:512 * (c_ + 1)], start=True, stop=True),
                    reads=[f"kscr:{c}", "pnT"], writes=[bn])
                E("tensor", lambda e, c_=c, b_=b: e.matmul(
                    b_[64:128, :], V["pnT"][64:128, :],
                    V["k_scr"][64:128, 512 * c_:512 * (c_ + 1)], start=True, stop=True),
                    reads=[f"kscr:{c}", "pnT"], writes=[bn])
                E("vector", lambda e, c_=c, b_=b: e.tensor_scalar(
                    V["kf_scr"][:, 512 * c_:512 * (c_ + 1)], b_[:, :], 0.0, EPS_AT,
                    ALU.max, ALU.add), reads=[bn], writes=[f"kf:{c}"])
            if do_dump:
                def dump4(srcv, srcnames, row0, tag):
                    for blk in range(4):
                        E("gpsimd", lambda e, b_=blk, s=srcv, r=row0: e.dma_start(
                            out=dbg[r + 128 * b_:r + 128 * (b_ + 1), :],
                            in_=s[:, 1024 * b_:1024 * (b_ + 1)]),
                            kind="d", reads=list(srcnames), writes=[f"{tag}{blk}"])
                dump4(V["k_scr"], [f"kscr:{i}" for i in range(DC)], 0, "dks")
            if SAFE:
                kb.barrier()
            if do_dump:
                if os.environ.get("DBGW"):
                    for blk in range(4):
                        E("gpsimd", lambda e, b_=blk: e.dma_start(
                            out=dbg[512 + 128 * b_:512 + 128 * (b_ + 1), :],
                            in_=V["wB"][:, 2048 * b_:2048 * (b_ + 1)][:, 0:1024]),
                            kind="d", reads=[f"wB:{2*blk}"], writes=[f"dwb{blk}"])
                    E("gpsimd", lambda e: e.dma_start(
                        out=dbg[2045:2046, 0:128], in_=V["onesb"][:, :]),
                        kind="d", reads=["onesb"], writes=["dob"])
                    E("gpsimd", lambda e: e.dma_start(
                        out=dbg[2046:2047, 0:1024], in_=V["bvrow"][:, :]),
                        kind="d", reads=["bvrow"], writes=["dbv"])
                else:
                    dump4(V["kf_scr"], [f"kf:{i}" for i in range(DC)], 512, "dkf")
            if SAFE:
                kb.barrier()
            for ts in range(4):
                pt, ptn = (ptA, "ptA") if ts % 2 == 0 else (ptB, "ptB")
                for c in range(DC):
                    E("tensor", lambda e, c_=c, t_=ts, p=pt: e.transpose(
                        p[:, 128 * c_:128 * (c_ + 1)],
                        V["kf_scr"][:, 512 * c_ + 128 * t_:512 * c_ + 128 * (t_ + 1)],
                        V["ident_bf"][:, :]),
                        reads=[f"kf:{c}", "identb"], writes=[ptn])
                E("scalar", lambda e, t_=ts, p=pt: e.activation(
                    V["kftm"][:, 1024 * t_:1024 * (t_ + 1)], p[:, :], AF.Identity),
                    reads=[ptn], writes=[f"kftm:{ts}"])
            if do_dump:
                dump4(V["kftm"], [f"kftm:{i}" for i in range(4)], 1024, "dkt")
            if SAFE:
                kb.barrier()
            for ts in range(4):
                tg = 4 * tt + ts
                for mb in range(2):
                    bi = (2 * ts + mb) % 4
                    b, bn = pm[bi], f"pm{bi}"
                    for kc in range(DC):
                        E("tensor", lambda e, k=kc, t_=tg, m=mb, b_=b: e.matmul(
                            b_[:, :], fmA(k)[:, 128 * t_:128 * (t_ + 1)],
                            wBc(k)[:, 512 * m:512 * (m + 1)],
                            start=(k == 0), stop=False),
                            reads=[f"wB:{kc}", f"fmA:{kc}"], writes=[bn])
                    E("tensor", lambda e, m=mb, b_=b: e.matmul(
                        b_[:, :], V["onesb"][:, :], V["bvrow"][:, 512 * m:512 * (m + 1)],
                        start=False, stop=True),
                        reads=["onesb", "bvrow"], writes=[bn])
                    E("scalar", lambda e, t_=ts, m=mb, b_=b: e.activation(
                        V["vtm"][:, 1024 * t_ + 512 * m:1024 * t_ + 512 * (m + 1)],
                        b_[:, :], AF.Identity), reads=[bn], writes=[f"vtm:{ts}"])
            if do_dump:
                for blk in range(3):
                    E("gpsimd", lambda e, b_=blk: e.dma_start(
                        out=dbg[1536 + 128 * b_:1536 + 128 * (b_ + 1), :],
                        in_=V["vtm"][:, 1024 * b_:1024 * (b_ + 1)]),
                        kind="d", reads=[f"vtm:{blk}"], writes=[f"dvt{blk}"])
            if SAFE:
                kb.barrier()
            for ts in range(4):
                for c in range(DC):
                    E("tensor", lambda e, c_=c, t_=ts,
                      f=(tt == 0 and ts == 0 and c == 0),
                      l=(tt == NTT - 1 and ts == 3 and c == DC - 1): e.matmul(
                        paux[:, 8 + c_:9 + c_],
                        V["kftm"][:, 1024 * t_ + 128 * c_:1024 * t_ + 128 * (c_ + 1)],
                        V["onecb"][:, :], start=f, stop=l),
                        reads=[f"kftm:{ts}", "onecb"], writes=["pksum"])
                    for hh in range(2):
                        first = (tt == 0 and ts == 0 and c == 0)
                        last = (tt == NTT - 1 and ts == 3)
                        rows = slice(64 * hh, 64 * (hh + 1))
                        E("tensor", lambda e, c_=c, t_=ts, h_=hh, r=rows,
                          f=first, l=last: e.matmul(
                            pkv[r, 64 * c_:64 * (c_ + 1)],
                            V["kftm"][:, 1024 * t_ + 128 * c_ + 64 * h_:
                                      1024 * t_ + 128 * c_ + 64 * (h_ + 1)],
                            V["vtm"][:, 1024 * t_ + 128 * c_ + 64 * h_:
                                     1024 * t_ + 128 * c_ + 64 * (h_ + 1)],
                            start=f, stop=l),
                            reads=[f"kftm:{ts}", f"vtm:{ts}"], writes=["pkv"])
        E("vector", lambda e: e.tensor_copy(V["kv_sb"][:, :], pkv[:, :]),
          reads=["pkv"], writes=["kv"])
        E("vector", lambda e: e.tensor_copy(V["ksum"][:, :], paux[:, 8:16]),
          reads=["pksum"], writes=["ksum"])
        if dbg is not None and not os.environ.get("DBGP2"):
            E("gpsimd", lambda e: e.dma_start(out=dbg[1920:2048, 0:512],
              in_=V["kv_sb"][:, :]), kind="d", reads=["kv"], writes=["dkv"])
            E("gpsimd", lambda e: e.dma_start(out=dbg[1920:2048, 512:520],
              in_=V["ksum"][:, :]), kind="d", reads=["ksum"], writes=["dksum"])
            return
        kb.barrier()

        wprep(H["wq"], wAc, "wA")
        wprep(H["wo"], wBc, "wB")
        E("vector", lambda e: e.memset(V["bd"][:, :], 0.0), writes=["bd"])
        for c in range(DC):
            E("vector", lambda e, c_=c: e.tensor_copy(
                V["bd"][0:64, 128 * c_:128 * c_ + 64],
                V["ksum"][0:64, c_:c_ + 1].broadcast_to((64, 64))),
                reads=["ksum"], writes=["bd"])
            E("vector", lambda e, c_=c: e.tensor_copy(
                V["bd"][64:128, 128 * c_ + 64:128 * (c_ + 1)],
                V["ksum"][64:128, c_:c_ + 1].broadcast_to((64, 64))),
                reads=["ksum"], writes=["bd"])
        kb.barrier()

        P2D = dbg is not None and os.environ.get("DBGP2")
        for tt in range(NTT):
            dump2 = P2D and tt == 0
            for mc in range(DC):
                b, bn = pm[mc % 2], f"pm{mc % 2}"
                for kc in range(DC):
                    E("tensor", lambda e, k=kc, m=mc, t=tt, b_=b: e.matmul(
                        b_[:, :], wAc(k)[:, 128 * m:128 * (m + 1)],
                        fmA(k)[:, 512 * t:512 * (t + 1)],
                        start=(k == 0), stop=(k == DC - 1)),
                        reads=[f"wA:{kc}", f"fmA:{kc}"], writes=[bn])
                E("scalar", lambda e, m=mc, b_=b: e.activation(
                    V["k_scr"][:, 512 * m:512 * (m + 1)], b_[:, :], AF.Identity,
                    bias=V["bq_sb"][:, m:m + 1]),
                    reads=[bn, "bq"], writes=[f"kscr:{mc}"])
            for c in range(DC):
                b, bn = pm[2 + c % 2], f"pm{2 + c % 2}"
                E("tensor", lambda e, c_=c, b_=b: e.matmul(
                    b_[0:64, :], V["pnT"][0:64, :],
                    V["k_scr"][0:64, 512 * c_:512 * (c_ + 1)], start=True, stop=True),
                    reads=[f"kscr:{c}", "pnT"], writes=[bn])
                E("tensor", lambda e, c_=c, b_=b: e.matmul(
                    b_[64:128, :], V["pnT"][64:128, :],
                    V["k_scr"][64:128, 512 * c_:512 * (c_ + 1)], start=True, stop=True),
                    reads=[f"kscr:{c}", "pnT"], writes=[bn])
                E("vector", lambda e, c_=c, b_=b: e.tensor_scalar(
                    V["kf_scr"][:, 512 * c_:512 * (c_ + 1)], b_[:, :], 0.0, EPS_AT,
                    ALU.max, ALU.add), reads=[bn], writes=[f"kf:{c}"])
            if dump2:
                for blk in range(4):
                    E("gpsimd", lambda e, b_=blk: e.dma_start(
                        out=dbg[1024 + 128 * b_:1024 + 128 * (b_ + 1), :],
                        in_=V["k_scr"][:, 1024 * b_:1024 * (b_ + 1)]),
                        kind="d", reads=[f"kscr:{2*blk}", f"kscr:{2*blk+1}"], writes=[f"dq{blk}"])
                for blk in range(4):
                    E("gpsimd", lambda e, b_=blk: e.dma_start(
                        out=dbg[512 + 128 * b_:512 + 128 * (b_ + 1), :],
                        in_=V["kf_scr"][:, 1024 * b_:1024 * (b_ + 1)]),
                        kind="d", reads=[f"kf:{2*blk}", f"kf:{2*blk+1}"], writes=[f"dqf{blk}"])
                E("gpsimd", lambda e: e.dma_start(
                    out=dbg[1664:1792, :], in_=V["bd"][:, 0:1024]),
                    kind="d", reads=["bd"], writes=["dbd"])
                E("gpsimd", lambda e: e.dma_start(
                    out=dbg[1792:1920, 0:512], in_=V["kv_sb"][:, :]),
                    kind="d", reads=["kv"], writes=["dkv2"])
            for c in range(DC):
                bdn, bdnn = pm[c % 2], f"pm{c % 2}"
                E("tensor", lambda e, c_=c, b_=bdn: e.matmul(
                    b_[:, :], V["bd"][:, 128 * c_:128 * (c_ + 1)],
                    V["kf_scr"][:, 512 * c_:512 * (c_ + 1)], start=True, stop=True),
                    reads=["bd", f"kf:{c}"], writes=[bdnn])
                if dump2 and c == 0:
                    E("vector", lambda e, b_=bdn: e.tensor_copy(
                        V["qscr"][:, 0:512], b_[:, :]), reads=[bdnn], writes=["qscr"])
                    E("gpsimd", lambda e: e.dma_start(
                        out=dbg[1536:1664, 0:512], in_=V["qscr"][:, 0:512]),
                        kind="d", reads=["qscr"], writes=["dden"])
                rd, rdn = ((V["rden"], "rden") if c % 2 == 0
                           else (V["rdenb"], "rdenb"))
                E("vector", lambda e, b_=bdn, r=rd: e.tensor_scalar(
                    r[:, :], b_[:, :], EPS_AT, None, ALU.add),
                    reads=[bdnn], writes=[rdn])
                E("vector", lambda e, r=rd: e.reciprocal(r[:, :], r[:, :]),
                  reads=[rdn], writes=[rdn], late=True)
                bq_, bqn = pm[2 + c % 2], f"pm{2 + c % 2}"
                E("tensor", lambda e, c_=c, b_=bq_: e.matmul(
                    b_[0:64, :], V["kv_sb"][0:64, 64 * c_:64 * (c_ + 1)],
                    V["kf_scr"][0:64, 512 * c_:512 * (c_ + 1)], start=True, stop=True),
                    reads=["kv", f"kf:{c}"], writes=[bqn])
                E("tensor", lambda e, c_=c, b_=bq_: e.matmul(
                    b_[64:128, :], V["kv_sb"][64:128, 64 * c_:64 * (c_ + 1)],
                    V["kf_scr"][64:128, 512 * c_:512 * (c_ + 1)], start=True, stop=True),
                    reads=["kv", f"kf:{c}"], writes=[bqn])
                if dump2 and c == 0:
                    E("gpsimd", lambda e: e.dma_start(
                        out=dbg[1536:1664, 512:1024], in_=V["rden"][:, :]),
                        kind="d", reads=["rden"], writes=["drden"])
                    E("vector", lambda e, b_=bq_: e.tensor_copy(
                        V["qscr"][:, 0:512], b_[:, :]), reads=[bqn], writes=["qscr"])
                    E("gpsimd", lambda e: e.dma_start(
                        out=dbg[1920:2048, 512:1024], in_=V["qscr"][:, 0:512]),
                        kind="d", reads=["qscr"], writes=["dnum"])
                E("vector", lambda e, c_=c, b_=bq_, r=rd: e.tensor_tensor(
                    V["attn_scr"][:, 512 * c_:512 * (c_ + 1)], b_[:, :], r[:, :],
                    ALU.mult), reads=[bqn, rdn], writes=[f"at:{c}"])
            for ts in range(4):
                tg = 4 * tt + ts
                for mb in range(2):
                    bi = (2 * ts + mb) % 4
                    b, bn = pm[bi], f"pm{bi}"
                    for c in range(DC):
                        E("tensor", lambda e, c_=c, t_=ts, m=mb, b_=b: e.matmul(
                            b_[:, :],
                            V["attn_scr"][:, 512 * c_ + 128 * t_:512 * c_ + 128 * (t_ + 1)],
                            wBc(c_)[:, 512 * m:512 * (m + 1)],
                            start=(c_ == 0), stop=False),
                            reads=[f"at:{c}", f"wB:{c}"], writes=[bn])
                    E("tensor", lambda e, m=mb, b_=b: e.matmul(
                        b_[:, :], V["onesb"][:, :], V["borow"][:, 512 * m:512 * (m + 1)],
                        start=False, stop=True),
                        reads=["onesb", "borow"], writes=[bn])
                    E("vector", lambda e, t_=tg, m=mb, b_=b: e.tensor_add(
                        xres_t(t_)[:, 512 * m:512 * (m + 1)], b_[:, :],
                        xres_t(t_)[:, 512 * m:512 * (m + 1)]),
                        reads=[bn, f"x:{tg}"], writes=[f"x:{tg}"])
                E("gpsimd", lambda e, t_=tg: e.dma_start(
                    out=H["x2s"][128 * t_:128 * (t_ + 1), :], in_=xres_t(t_)),
                    kind="d", reads=[f"x:{tg}"], writes=[f"x2s:{tg}"])
                if dbg is None:
                    ln_stats(tg, xres_t(tg), f"x:{tg}", absmax=True)
        kb.barrier()

    E("scalar", lambda e: e.activation(sr(18), V["clip_sb"][:, 0:1], AF.Sigmoid),
      reads=["clip"], writes=["srow"])
    E("scalar", lambda e: e.activation(sr(34), V["clip_sb"][:, 1:2], AF.Sigmoid),
      reads=["clip2"], writes=["srow"])

    if PHASE == "dbg1":
        for tc in range(NT):
            ln_stats(tc, xres_t(tc), f"x:{tc}", absmax=True)
        E("vector", lambda e: e.reduce_max(V["red2"][:, :], V["lnmax"][:, :], axis=AX.X),
          reads=["lnmax"], writes=["red2"], late=True)
        partition_max(V["red2"][:, :], "red2", sr(2), "srow")
        global_max(0, sr(2), "srow", sr(3), "srow")
        scale_chain(3, 18, 4, 5, bc_inv=0, bc_s=1)
        E("vector", lambda e: e.tensor_scalar(
            V["AQ"][:, :], V["A_st"][:, :], bc(0), None, ALU.mult),
            reads=["A_st", "bcol"], writes=["AQ"])
        E("vector", lambda e: e.tensor_scalar(
            V["BQ"][:, :], V["B_st"][:, :], bc(0), None, ALU.mult),
            reads=["B_st", "bcol"], writes=["BQ"])
        E("gpsimd", lambda e: e.dma_start(out=H["out"][0:128, 0:16], in_=V["A_st"][:, :]),
          kind="d", reads=["A_st"], writes=["o1"])
        E("gpsimd", lambda e: e.dma_start(out=H["out"][0:128, 16:32], in_=V["B_st"][:, :]),
          kind="d", reads=["B_st"], writes=["o2"])
        E("gpsimd", lambda e: e.dma_start(out=H["out"][0:128, 32:48], in_=V["AQ"][:, :]),
          kind="d", reads=["AQ"], writes=["o3"])
        E("gpsimd", lambda e: e.dma_start(out=H["out"][0:128, 48:64], in_=V["BQ"][:, :]),
          kind="d", reads=["BQ"], writes=["o4"])
        E("gpsimd", lambda e: e.dma_start(out=H["out"][0:128, 64:80], in_=V["lnmax"][:, :]),
          kind="d", reads=["lnmax"], writes=["o5"])
        E("gpsimd", lambda e: e.dma_start(out=H["out"][300:301, 0:64], in_=V["srow"][:, :]),
          kind="d", reads=["srow"], writes=["o6"])
        E("gpsimd", lambda e: e.dma_start(out=H["out"][301:302, 0:8], in_=V["grow"][:, :]),
          kind="d", reads=["grow"], writes=["o7"])
        E("gpsimd", lambda e: e.dma_start(out=H["out"][310:438, 0:32], in_=V["bcol"][:, :]),
          kind="d", reads=["bcol"], writes=["o8"])
        E("gpsimd", lambda e: e.dma_start(out=H["out"][302:303, 0:2], in_=V["clip_sb"][:, :]),
          kind="d", reads=["clip", "clip2"], writes=["o9"])
        return

    if PHASE == "xq1":
        for tc in range(NT):
            ln_stats(tc, xres_t(tc), f"x:{tc}", absmax=True)
        E("vector", lambda e: e.reduce_max(V["red2"][:, :], V["lnmax"][:, :], axis=AX.X),
          reads=["lnmax"], writes=["red2"], late=True)
        partition_max(V["red2"][:, :], "red2", sr(2), "srow")
        global_max(0, sr(2), "srow", sr(3), "srow")
        scale_chain(3, 18, 4, 5, bc_inv=0, bc_s=1)
        E("vector", lambda e: e.tensor_scalar(
            V["AQ"][:, :], V["A_st"][:, :], bc(0), None, ALU.mult),
            reads=["A_st", "bcol"], writes=["AQ"])
        E("vector", lambda e: e.tensor_scalar(
            V["BQ"][:, :], V["B_st"][:, :], bc(0), None, ALU.mult),
            reads=["B_st", "bcol"], writes=["BQ"])
        for tc in range(NT):
            E("scalar", lambda e, t=tc: e.activation(
                V["qscr"][:, :], xres_t(t), AF.Identity,
                bias=V["BQ"][:, t:t + 1], scale=V["AQ"][:, t:t + 1]),
                reads=[f"x:{tc}", "AQ", "BQ"], writes=["qscr"])
            E("vector", lambda e: e.tensor_scalar(
                V["i32"][:, :], V["qscr"][:, :], 127.49, -128.49, ALU.min, ALU.max),
                reads=["qscr"], writes=["i32s"])
            E("vector", lambda e: e.tensor_copy(V["qscr"][:, :], V["i32"][:, :]),
              reads=["i32s"], writes=["qscr"])
            E("gpsimd", lambda e, t=tc: e.dma_start(
                out=H["out"][128 * t:128 * (t + 1), :], in_=V["qscr"][:, :]),
                kind="d", reads=["qscr"], writes=[f"o:{tc}"])
        return

    dst1 = H["out"] if PHASE == "x1" else None

    def ln2_hook(tc):
        if PHASE not in ("x1",):
            ln_stats(tc, xres_t(tc), f"x:{tc}")

    for tc in range(NT):
        ln_stats(tc, xres_t(tc), f"x:{tc}", absmax=True)
    E("vector", lambda e: e.reduce_max(V["red2"][:, :], V["lnmax"][:, :], axis=AX.X),
      reads=["lnmax"], writes=["red2"], late=True)
    partition_max(V["red2"][:, :], "red2", sr(2), "srow")
    global_max_launch(0, sr(2), "srow")
    prep_marks = emit_coop_prep()
    prep_reads = ([f"myg:{s}" for s in range(6)] + ["mysc"]
                  + [f"mywd:{j}" for j in range(JC)])
    emit_ffn(True, H["x_in"], dst1, sig_idx=18, sgi=2, tag="ffn1",
             post_tile_hook=ln2_hook, prelaunched=True,
             quant_guard=prep_marks, quant_guard_reads=prep_reads)
    if PHASE == "x1":
        E("gpsimd", lambda e: e.dma_start(out=H["dbg"][0:1, :], in_=V["srow"][:, :]),
          kind="d", reads=["srow"], writes=["dbg"])
        E("gpsimd", lambda e: e.dma_start(out=H["dbg"][1:2, 0:32], in_=V["bcol"][0:1, 0:32]),
          kind="d", reads=["bcol"], writes=["dbg"])
        E("gpsimd", lambda e: e.dma_start(out=H["dbg"][1:2, 32:64], in_=V["onesb"][:, 0:32]),
          kind="d", reads=["onesb"], writes=["dbg"])
        return
    kb.barrier()

    if PHASE == "dbga":
        emit_attn(dbg=H["out"] if not os.environ.get("DBGP2") else None)
        return
    if PHASE == "dbga2":
        emit_attn(dbg=H["out"])
        return

    emit_attn(skip_stats=(PHASE not in ("x1",)))
    if PHASE == "x2":
        for tc in range(NT):
            E("gpsimd", lambda e, t=tc: e.dma_start(
                out=H["out"][128 * t:128 * (t + 1), :], in_=xres_t(t)),
                kind="d", reads=[f"x:{tc}"], writes=[f"o:{tc}"])
        return

    emit_ffn(True, H["x2s"], H["out"], sig_idx=18, sgi=40, tag="ffn3",
             src_names="x2s:{}", skip_stats=True)
    E("gpsimd", lambda e: e.dma_start(out=H["dbg"][0:1, :], in_=V["srow"][:, :]),
      kind="d", reads=["srow"], writes=["dbg"])
    E("gpsimd", lambda e: e.dma_start(out=H["dbg"][1:2, 0:32], in_=V["bcol"][0:1, 0:32]),
      kind="d", reads=["bcol"], writes=["dbg"])



def kernel(**inputs):
    x = np.ascontiguousarray(np.asarray(inputs["L0_attn_x"], np.float32))
    B = x.shape[0]

    def bias_cols(b):
        return np.ascontiguousarray(np.asarray(b, np.float32).reshape(DC, 128).T)

    common = {
        "wq": np.ascontiguousarray(np.asarray(inputs["L0_attn_Wq"], np.float32)),
        "wk": np.ascontiguousarray(np.asarray(inputs["L0_attn_Wk"], np.float32)),
        "wv": np.ascontiguousarray(np.asarray(inputs["L0_attn_Wv"], np.float32)),
        "wo": np.ascontiguousarray(np.asarray(inputs["L0_attn_Wo"], np.float32)),
        "bq": bias_cols(inputs["L0_attn_bq"]),
        "bk": bias_cols(inputs["L0_attn_bk"]),
        "bvr": np.asarray(inputs["L0_attn_bv"], np.float32).reshape(1, D).copy(),
        "bor": np.asarray(inputs["L0_attn_bo"], np.float32).reshape(1, D).copy(),
        "pT": np.ascontiguousarray(np.asarray(inputs["L0_attn_P"], np.float32).T),
        "clip_x": np.asarray(inputs["L0_ffn_clip_x"], np.float32).reshape(1, 1).copy(),
        "clip_down": np.asarray(inputs["L0_ffn_clip_down"], np.float32).reshape(1, 1).copy(),
    }
    wg = np.asarray(inputs["L0_ffn_w_gate"], np.float32)
    wu = np.asarray(inputs["L0_ffn_w_up"], np.float32)
    wd = np.asarray(inputs["L0_ffn_w_down"], np.float32)
    nc = build_program()
    in_maps = []
    for b in range(NCORES):
        mm = dict(common)
        mm["x"] = np.ascontiguousarray(x[b % B])
        jl = [b, 8 + b, min(16 + b, JC - 1)]
        mm["wg_mine"] = np.ascontiguousarray(
            np.concatenate([wg[128 * j:128 * (j + 1)] for j in jl], axis=0))
        mm["wu_mine"] = np.ascontiguousarray(
            np.concatenate([wu[128 * j:128 * (j + 1)] for j in jl], axis=0))
        mm["wd_mine"] = np.ascontiguousarray(wd[128 * b:128 * (b + 1)])
        in_maps.append(mm)
    res = run_bass_kernel_spmd(nc, in_maps, core_ids=list(range(NCORES)))
    out = np.stack([res.results[b]["out"] for b in range(min(B, NCORES))], axis=0)
    try:
        kernel.dbg = np.stack([res.results[b]["dbg"] for b in range(NCORES)], axis=0)
    except Exception:
        kernel.dbg = None
    return out.astype(np.float32)


# version 13
# speedup vs baseline: 1.0197x; 1.0197x over previous
import os
import math
import numpy as np
import concourse.bass as bass
import concourse.mybir as mybir
from concourse.bass_utils import run_bass_kernel_spmd

F32 = mybir.dt.float32
BF16 = mybir.dt.bfloat16
I32 = mybir.dt.int32
U8 = mybir.dt.uint8
AF = mybir.ActivationFunctionType
ALU = mybir.AluOpType
AX = mybir.AxisListType

PHASE = "full"
NCORES = 8

T, D, II = 2048, 1024, 2816
NH, DK = 16, 64
NT, NTT, DC, JC = 16, 4, 8, 22
EPS_LN, EPS_AT, QMAX = 1e-5, 1e-6, 127.0
NORM = 1.0 / math.sqrt(math.sqrt(DK))

ENGINES = ("sync", "vector", "scalar", "tensor", "gpsimd")
AUDIT = {}
DMA_POOL = 8
ESEM = {"vector": "VE", "scalar": "SE", "tensor": "PE", "gpsimd": "GE"}
DSEM = {"sync": "SXD", "gpsimd": "GDD"}


class KB:
    def __init__(self, dummy_builders=None):
        self.ops = {e: [] for e in ENGINES}
        self.rw = {}
        self.bar = {e: {} for e in ENGINES}
        self.dummy = dummy_builders or {}

    def _dep_add(self, deps, eng, e2, i2):
        if e2 == eng:
            if self.ops[e2][i2]["kind"] == "c" and e2 != "gpsimd":
                return
        if deps.get(e2, -1) < i2:
            deps[e2] = i2

    SMALL = {"srow", "red1", "red2", "mv", "grow", "ksum", "bcol", "A_st",
             "B_st", "AQ", "BQ", "wsg", "habs", "lnmax", "stat6", "cst"}

    def emit(self, eng, build, reads=(), writes=(), kind="c", extra=(),
             late=False):
        if not late and any(b in self.SMALL for b in writes):
            late = True
        need_gap = 0
        n = len(self.ops[eng])
        for b in reads:
            st = self.rw.get(b)
            if st:
                i2 = st["w"].get(eng)
                if i2 is not None and self.ops[eng][i2].get("late"):
                    d = n - 1 - i2
                    if d < 2:
                        need_gap = max(need_gap, 2 - d)
        for _ in range(need_gap):
            if eng in self.dummy:
                self.ops[eng].append({"build": self.dummy[eng], "deps": {},
                                      "kind": "c"})
                n += 1
        idx = len(self.ops[eng])
        deps = {}
        for e2, i2 in self.bar[eng].items():
            self._dep_add(deps, eng, e2, i2)
        self.bar[eng] = {}
        for b in reads:
            st = self.rw.get(b)
            if st:
                for e2, i2 in st["w"].items():
                    self._dep_add(deps, eng, e2, i2)
        for b in writes:
            st = self.rw.get(b)
            if st:
                for e2, i2 in st["w"].items():
                    self._dep_add(deps, eng, e2, i2)
                for e2, i2 in st["r"].items():
                    self._dep_add(deps, eng, e2, i2)
        for e2, i2 in extra:
            self._dep_add(deps, eng, e2, i2)
        self.ops[eng].append({"build": build, "deps": deps, "kind": kind,
                              "late": late})
        for b in reads:
            self.rw.setdefault(b, {"w": {}, "r": {}})["r"][eng] = idx
        for b in writes:
            self.rw[b] = {"w": {eng: idx}, "r": {}}
        return (eng, idx)

    def mark(self, eng):
        return (eng, len(self.ops[eng]) - 1)

    def barrier(self):
        last = {e: len(self.ops[e]) - 1 for e in ENGINES if self.ops[e]}
        for e in ENGINES:
            for e2, i2 in last.items():
                if e2 != e and self.bar[e].get(e2, -1) < i2:
                    self.bar[e][e2] = i2

    def resolve(self):
        marked = {e: set() for e in ENGINES}
        for e in ENGINES:
            for op in self.ops[e]:
                for e2, i2 in op["deps"].items():
                    if self.ops[e2][i2]["kind"] == "c":
                        marked[e2].add(i2)
        for e in ENGINES:
            for i in range(len(self.ops[e]) - 1, -1, -1):
                if self.ops[e][i]["kind"] == "c":
                    marked[e].add(i)
                    break
        self.rank = {e: [0] * len(self.ops[e]) for e in ENGINES}
        self.is_marked = {e: [False] * len(self.ops[e]) for e in ENGINES}
        self.dma_rank = {e: [0] * len(self.ops[e]) for e in ENGINES}
        self.cc_rank = {e: [0] * len(self.ops[e]) for e in ENGINES}
        for e in ENGINES:
            nm = nd = ncc = 0
            for i, op in enumerate(self.ops[e]):
                if op["kind"] == "c" and i in marked[e]:
                    nm += 1
                    self.is_marked[e][i] = True
                if op["kind"] == "d":
                    nd += 1
                if op["kind"] == "cc":
                    ncc += 1
                self.rank[e][i] = nm
                self.dma_rank[e][i] = nd
                self.cc_rank[e][i] = ncc
        P = DMA_POOL
        self.waits = {e: [None] * len(self.ops[e]) for e in ENGINES}
        for e in ENGINES:
            hi = {}
            for i, op in enumerate(self.ops[e]):
                ws = []
                if op["kind"] == "d":
                    r = self.dma_rank[e][i] - 1
                    sem = f"{DSEM[e]}{r % P}"
                    val = 16 * (r // P)
                    if val > 0 and hi.get(sem, 0) < val:
                        hi[sem] = val
                        ws.append((sem, val))
                for e2, i2 in sorted(op["deps"].items()):
                    k2 = self.ops[e2][i2]["kind"]
                    if k2 == "c":
                        sem, val = ESEM[e2], self.rank[e2][i2]
                    elif k2 == "d":
                        r2 = self.dma_rank[e2][i2] - 1
                        sem, val = f"{DSEM[e2]}{r2 % P}", 16 * (r2 // P + 1)
                    else:
                        sem, val = "CC", self.cc_rank[e2][i2]
                    if val > 0 and hi.get(sem, 0) < val:
                        hi[sem] = val
                        ws.append((sem, val))
                self.waits[e][i] = ws

    def replay(self, e, engine, sems):
        P = DMA_POOL
        for i, op in enumerate(self.ops[e]):
            for sem, val in self.waits[e][i]:
                engine.wait_ge(sems[sem], val)
            instr = op["build"](engine)
            if op["kind"] == "d":
                r = self.dma_rank[e][i] - 1
                instr.then_inc(sems[f"{DSEM[e]}{r % P}"], 16)
            elif op["kind"] == "cc":
                instr.then_inc(sems["CC"], 1)
            elif self.is_marked[e][i]:
                instr.then_inc(sems[ESEM[e]], 1)
        if e in DSEM and self.ops[e]:
            nd = self.dma_rank[e][len(self.ops[e]) - 1]
            for q in range(min(P, nd)):
                tot = (nd - 1 - q) // P + 1
                engine.wait_ge(sems[f"{DSEM[e]}{q}"], 16 * tot)
        if e == "gpsimd" and self.ops[e]:
            ncc = self.cc_rank[e][len(self.ops[e]) - 1]
            if ncc:
                engine.wait_ge(sems["CC"], ncc)


def build_program():
    nc = bass.Bass("TRN2", target_bir_lowering=False)

    x_in = nc.dram_tensor("x", [T, D], F32, kind="ExternalInput").ap()
    H = {"x_in": x_in}
    for n, shp in (("wq", [D, D]), ("wk", [D, D]), ("wv", [D, D]), ("wo", [D, D]),
                   ("bq", [128, DC]), ("bk", [128, DC]),
                   ("bvr", [1, D]), ("bor", [1, D]), ("pT", [DK, DK]),
                   ("clip_x", [1, 1]), ("clip_down", [1, 1])):
        H[n] = nc.dram_tensor(n, shp, F32, kind="ExternalInput").ap()
    H["out"] = nc.dram_tensor("out", [T, D], F32, kind="ExternalOutput").ap()
    H["dbg"] = nc.dram_tensor("dbg", [2, 64], F32, kind="ExternalOutput").ap()
    H["x2s"] = nc.dram_tensor("x2s", [T, D], F32, kind="Internal").ap()
    H["wg_mine"] = nc.dram_tensor("wg_mine", [384, 1024], F32,
                                  kind="ExternalInput").ap()
    H["wu_mine"] = nc.dram_tensor("wu_mine", [384, 1024], F32,
                                  kind="ExternalInput").ap()
    H["wd_mine"] = nc.dram_tensor("wd_mine", [128, II], F32,
                                  kind="ExternalInput").ap()
    for i in range(3):
        H[f"myg{i}"] = nc.dram_tensor(f"myg{i}", [256, 1024], BF16,
                                      kind="Internal").ap()
        H[f"allg{i}"] = nc.dram_tensor(f"allg{i}", [2048, 1024], BF16,
                                       kind="Internal", addr_space="Shared").ap()
    H["mywd"] = nc.dram_tensor("mywd", [II, 128], BF16, kind="Internal").ap()
    H["allwd"] = nc.dram_tensor("allwd", [8 * II, 128], BF16, kind="Internal",
                                addr_space="Shared").ap()
    H["wdT_h"] = nc.dram_tensor("wdT_h", [JC * 128, 1024], BF16,
                                kind="Internal").ap()
    H["mysc"] = nc.dram_tensor("mysc", [128, 8], F32, kind="Internal").ap()
    H["allsc"] = nc.dram_tensor("allsc", [1024, 8], F32, kind="Internal",
                                addr_space="Shared").ap()
    H["cc_in"] = [nc.dram_tensor(f"cc_in{i}", [1, 1], F32, kind="Internal").ap()
                  for i in range(4)]
    H["cc_out"] = [nc.dram_tensor(f"cc_out{i}", [1, 8], F32, kind="Internal",
                                  addr_space="Shared").ap()
                   for i in range(4)]

    AB = 212480
    OFF = {}
    _o = [0]

    def rsv(name, nbytes):
        OFF[name] = _o[0]
        _o[0] += nbytes
        assert _o[0] <= AB, (name, _o[0])

    rsv("x_res", 65536)
    rsv("actA", 32768)
    rsv("harena", 90112)
    rsv("i32", 4096)
    rsv("qscr", 4096)
    rsv("silu", 2048)
    for n, b in (("ident_bf", 256), ("ident_f32", 512), ("onesf", 512),
                 ("onesb", 256), ("onecb", 8), ("A_st", 64), ("B_st", 64), ("AQ", 64),
                 ("BQ", 64), ("lnmax", 64), ("habs", 352), ("wsg", 352),
                 ("stat6", 48), ("mv", 8), ("red1", 4), ("red2", 4),
                 ("bcol", 336), ("wsga", 256), ("srow", 256), ("grow", 32), ("bq_sb", 32),
                 ("bk_sb", 32), ("pT_f32", 256), ("pnT", 128), ("clip_sb", 8),
                 ("ksum", 32), ("xin0", 4096), ("xin1", 4096), ("cst", 32)):
        rsv(n, b)
    XR, HA, XA = OFF["x_res"], OFF["harena"], OFF["actA"]
    for n, off in (("wgT0", 16384), ("wgT1", 18432),
                   ("wuT0", 20480), ("wuT1", 22528)):
        OFF[n] = XR + off
    for n, off in (("pnat0", 0), ("pnat1", 4096), ("pwdn", 8192),
                   ("pq", 19456), ("pwdq", 19456), ("pstg0", 21504),
                   ("pstg1", 23552), ("pwdt", 0)):
        OFF[n] = XA + off
    for n, off in (("wA", 0), ("wB", 16384), ("k_scr", 32768), ("kf_scr", 40960),
                   ("kftm", 49152), ("vtm", 57344), ("attn_scr", 65536),
                   ("rden", 73728), ("bd", 75776), ("kv_sb", 77824),
                   ("bvrow", 78848), ("borow", 80896), ("wnat_a", 82944)):
        OFF[n] = HA + off
    assert OFF["wnat_a"] + 4096 <= HA + 90112

    V = {}

    def mkviews(base):
        def mk(name, shape, dt, off):
            h = nc.virtual_sbuf_tensor(name, shape, dt)
            nc._tensor(h.name, h.shape, h.dtype, type="SB")
            ml = nc.lookup_mloc(h)
            ml.allocated = True
            ml.addr = base + off
            AUDIT[h.name] = off
            V[name] = h.ap()

        mk("x_res", [128, 16 * 1024], F32, OFF["x_res"])
        mk("actA", [128, 8 * 2048], BF16, OFF["actA"])
        mk("harena", [128, 22 * 2048], BF16, OFF["harena"])
        mk("i32", [128, 1024], I32, OFF["i32"])
        mk("qscr", [128, 1024], F32, OFF["qscr"])
        mk("silu", [128, 512], F32, OFF["silu"])
        for n in ("wgT0", "wgT1", "wuT0", "wuT1"):
            mk(n, [128, 1024], BF16, OFF[n])
        for n in ("pnat0", "pnat1"):
            mk(n, [128, 1024], F32, OFF[n])
        mk("pwdn", [128, II], F32, OFF["pwdn"])
        mk("pq", [128, 1024], BF16, OFF["pq"])
        mk("pwdq", [128, II], BF16, OFF["pwdq"])
        for n in ("pstg0", "pstg1"):
            mk(n, [128, 1024], BF16, OFF[n])
        mk("pwdt", [128, II], BF16, OFF["pwdt"])
        mk("silu2", [128, 512], F32, XR + 0)
        for n in ("wA", "wB"):
            mk(n, [128, 8192], BF16, OFF[n])
        for n in ("k_scr", "kf_scr", "kftm", "vtm", "attn_scr"):
            mk(n, [128, 4096], BF16, OFF[n])
        mk("astage", [128, 1024], BF16, OFF["vtm"])
        mk("astage2", [128, 1024], BF16, OFF["vtm"] + 2048)
        mk("rden", [128, 512], F32, OFF["rden"])
        mk("wnat_a", [128, 1024], F32, OFF["wnat_a"])
        mk("wnat_b", [128, 1024], F32, OFF["k_scr"])
        mk("wdT16", [128, 1024], BF16, OFF["qscr"])
        mk("wdT17", [128, 1024], BF16, OFF["qscr"] + 2048)
        mk("wdT18", [128, 1024], BF16, OFF["i32"])
        mk("wdT19", [128, 1024], BF16, OFF["i32"] + 2048)
        mk("wdT20", [128, 1024], BF16, OFF["silu"])
        mk("wdT21", [128, 1024], BF16, OFF["xin1"])
        mk("i32h", [128, 512], I32, OFF["xin1"] + 2048)
        mk("ident_bf", [128, 128], BF16, OFF["ident_bf"])
        mk("onecb", [128, 1], BF16, OFF["onecb"])
        mk("rdenb", [128, 512], F32, OFF["kftm"])
        mk("ident_f32", [128, 128], F32, OFF["ident_f32"])
        mk("onesf", [1, 128], F32, OFF["onesf"])
        mk("onesb", [1, 128], BF16, OFF["onesb"])
        for n, w in (("A_st", 16), ("B_st", 16), ("AQ", 16), ("BQ", 16),
                     ("lnmax", 16), ("habs", 88), ("wsg", 88), ("stat6", 12),
                     ("mv", 2), ("red1", 1), ("red2", 1),
                     ("bq_sb", 8), ("bk_sb", 8), ("ksum", 8)):
            mk(n, [128, w], F32, OFF[n])
        mk("bcol", [128, 84], F32, OFF["bcol"])
        mk("wsga", [128, 64], F32, OFF["wsga"])
        mk("srow", [1, 64], F32, OFF["srow"])
        mk("grow", [1, 8], F32, OFF["grow"])
        mk("pT_f32", [64, 64], F32, OFF["pT_f32"])
        mk("pnT", [128, 64], BF16, OFF["pnT"])
        mk("bvrow", [1, 1024], BF16, OFF["bvrow"])
        mk("borow", [1, 1024], BF16, OFF["borow"])
        mk("clip_sb", [1, 2], F32, OFF["clip_sb"])
        mk("bd", [128, 1024], BF16, OFF["bd"])
        mk("kv_sb", [128, 512], BF16, OFF["kv_sb"])
        mk("xin0", [128, 1024], F32, OFF["xin0"])
        mk("xin1", [128, 1024], F32, OFF["xin1"])
        mk("cst", [128, 8], F32, OFF["cst"])
        mk("dbgw", [128, 1384], F32, OFF["silu"])

    with (
        nc.sbuf_tensor("arena", [128, AB], U8) as arena_t,
        nc.psum_tensor("pm0", [128, 512], F32) as pm0,
        nc.psum_tensor("pm1", [128, 512], F32) as pm1,
        nc.psum_tensor("pm2", [128, 512], F32) as pm2,
        nc.psum_tensor("pm3", [128, 512], F32) as pm3,
        nc.psum_tensor("ptA", [128, 1024], BF16) as ptA,
        nc.psum_tensor("ptB", [128, 1024], BF16) as ptB,
        nc.psum_tensor("pkv", [128, 512], F32) as pkv,
        nc.psum_tensor("paux", [128, 512], F32) as paux,
        nc.semaphore("PE") as s2,
        nc.semaphore("VE") as s3,
        nc.semaphore("SE") as s4,
        nc.semaphore("GE") as s5,
        nc.semaphore("CC") as s6,
    ):
        import contextlib as _ctx
        _stack = _ctx.ExitStack()
        dma_sems = {}
        for qn in ("SXD", "GDD"):
            for q in range(DMA_POOL):
                dma_sems[f"{qn}{q}"] = _stack.enter_context(
                    nc.semaphore(f"{qn}{q}"))
        mkviews(nc.lookup_mloc(arena_t).addr)
        nc.const_aps.register(F32, 0.0, V["cst"][:, 0:1])
        nc.const_aps.register(F32, 1.0, V["cst"][:, 1:2])
        nc.const_aps.register(F32, EPS_AT, V["cst"][:, 2:3])
        sems = {"PE": s2, "VE": s3, "SE": s4, "GE": s5, "CC": s6}
        sems.update(dma_sems)
        _dctr = {"vector": 0, "scalar": 0}

        def _vdummy(e):
            c = 4 + (_dctr["vector"] % 4)
            _dctr["vector"] += 1
            return e.tensor_copy(V["cst"][:, c:c + 1], V["cst"][:, 1:2])

        def _sdummy(e):
            return e.copy(V["cst"][:, 3:4], V["cst"][:, 1:2])

        kb = KB(dummy_builders={"vector": _vdummy, "scalar": _sdummy})
        emit_all(kb, V, H, [pm0, pm1, pm2, pm3], ptA, ptB, pkv, paux)
        kb.resolve()
        with nc.Block() as block:
            @block.sync
            def _(e):
                kb.replay("sync", e, sems)

            @block.vector
            def _(e):
                kb.replay("vector", e, sems)

            @block.scalar
            def _(e):
                kb.replay("scalar", e, sems)

            @block.tensor
            def _(e):
                kb.replay("tensor", e, sems)

            @block.gpsimd
            def _(e):
                kb.replay("gpsimd", e, sems)
        _stack.close()
    return nc



def emit_all(kb, V, H, pm, ptA, ptB, pkv, paux):
    E = kb.emit

    def xres_t(tc):
        return V["x_res"][:, 1024 * tc:1024 * (tc + 1)]

    def fmA(c):
        return V["actA"][:, 2048 * c:2048 * (c + 1)]

    def h_c(j):
        return V["harena"][:, 2048 * j:2048 * (j + 1)]

    def wdT(j):
        if j < 16:
            return V["actA"][:, 1024 * j:1024 * (j + 1)]
        return V[f"wdT{j}"][:, :]

    def wdT_name(j):
        return f"wdT:{j}"

    def wAc(k):
        return V["wA"][:, 1024 * k:1024 * (k + 1)]

    def wBc(k):
        return V["wB"][:, 1024 * k:1024 * (k + 1)]

    E("gpsimd", lambda e: e.memset(V["cst"][:, 0:1], 0.0), writes=["cst"])
    E("gpsimd", lambda e: e.memset(V["cst"][:, 1:2], 1.0), writes=["cst"])
    kb.barrier()
    E("gpsimd", lambda e: e.memset(V["ident_bf"][:, :], 0.0), writes=["identb"])
    E("gpsimd", lambda e: e.affine_select(
        out=V["ident_bf"][:, :], in_=V["ident_bf"][:, :], compare_op=ALU.not_equal,
        fill=1.0, base=0, pattern=[[-1, 128]], channel_multiplier=1),
        reads=["identb"], writes=["identb"])
    E("gpsimd", lambda e: e.memset(V["ident_f32"][:, :], 0.0), writes=["identf"])
    E("gpsimd", lambda e: e.affine_select(
        out=V["ident_f32"][:, :], in_=V["ident_f32"][:, :], compare_op=ALU.not_equal,
        fill=1.0, base=0, pattern=[[-1, 128]], channel_multiplier=1),
        reads=["identf"], writes=["identf"])
    E("gpsimd", lambda e: e.memset(V["onesf"][:, :], 1.0), writes=["onesf"])
    E("gpsimd", lambda e: e.memset(V["onesb"][:, :], 1.0), writes=["onesb"])
    E("gpsimd", lambda e: e.memset(V["onecb"][:, :], 1.0), writes=["onecb"])
    E("gpsimd", lambda e: e.memset(V["cst"][:, 2:3], EPS_AT), writes=["cst"])

    E("sync", lambda e: e.dma_start(out=V["clip_sb"][:, 0:1], in_=H["clip_x"]),
      kind="d", writes=["clip"])
    E("sync", lambda e: e.dma_start(out=V["clip_sb"][:, 1:2], in_=H["clip_down"]),
      kind="d", writes=["clip2"])
    E("sync", lambda e: e.dma_start(out=V["bq_sb"][:, :], in_=H["bq"]),
      kind="d", writes=["bq"])
    E("sync", lambda e: e.dma_start(out=V["bk_sb"][:, :], in_=H["bk"]),
      kind="d", writes=["bk"])
    E("sync", lambda e: e.dma_start(out=V["pT_f32"][:, :], in_=H["pT"]),
      kind="d", writes=["pT"])
    for tc in range(NT):
        E("sync", lambda e, t=tc: e.dma_start(
            out=xres_t(t), in_=H["x_in"][128 * t:128 * (t + 1), :]),
            kind="d", writes=[f"x:{tc}"])
    E("vector", lambda e: e.tensor_scalar_mul(V["pnT"][0:64, :], V["pT_f32"][:, :], NORM),
      reads=["pT"], writes=["pnT"])
    E("vector", lambda e: e.tensor_scalar_mul(V["pnT"][64:128, :], V["pT_f32"][:, :], NORM),
      reads=["pT"], writes=["pnT"])

    def ln_stats(tc, xap, xname, absmax=False):
        E("vector", lambda e, x=xap: e.bn_stats(V["stat6"][:, 0:6], x[:, 0:512]),
          reads=[xname], writes=["stat6"], late=True)
        E("vector", lambda e, x=xap: e.bn_stats(V["stat6"][:, 6:12], x[:, 512:1024]),
          reads=[xname], writes=["stat6"], late=True)
        E("vector", lambda e: e.bn_aggr(V["mv"][:, :], V["stat6"][:, :]),
          reads=["stat6"], writes=["mv"], late=True)
        E("vector", lambda e: e.tensor_scalar_add(V["red1"][:, :], V["mv"][:, 1:2], EPS_LN),
          reads=["mv"], writes=["red1"])
        E("scalar", lambda e: e.activation(V["red2"][:, :], V["red1"][:, :], AF.Sqrt),
          reads=["red1"], writes=["red2"])
        E("vector", lambda e, t=tc: e.reciprocal(V["A_st"][:, t:t + 1], V["red2"][:, :]),
          reads=["red2"], writes=["A_st"], late=True)
        E("vector", lambda e, t=tc: e.scalar_tensor_tensor(
            V["B_st"][:, t:t + 1], V["mv"][:, 0:1], -1.0, V["A_st"][:, t:t + 1],
            ALU.mult, ALU.mult), reads=["mv", "A_st"], writes=["B_st"])
        if absmax:
            E("scalar", lambda e, x=xap, t=tc: e.activation(
                V["qscr"][:, :], x, AF.Abs,
                bias=V["B_st"][:, t:t + 1], scale=V["A_st"][:, t:t + 1]),
                reads=[xname, "A_st", "B_st"], writes=["qscr"])
            E("vector", lambda e, t=tc: e.reduce_max(
                V["lnmax"][:, t:t + 1], V["qscr"][:, :], axis=AX.X),
                reads=["qscr"], writes=["lnmax"], late=True)

    def partition_max(col_ap, colname, dst11, dstname):
        E("tensor", lambda e, c=col_ap: e.transpose(
            paux[0:1, 0:128], c, V["ident_f32"][:, :]),
            reads=[colname, "identf"], writes=["paux"])
        E("vector", lambda e, d=dst11: e.reduce_max(d, paux[0:1, 0:128], axis=AX.X),
          reads=["paux"], writes=[dstname], late=True)

    def global_max_launch(idx, src11, srcname):
        E("gpsimd", lambda e, i=idx, s=src11: e.dma_start(out=H["cc_in"][i], in_=s),
          kind="d", reads=[srcname], writes=[f"ccin{idx}"])
        E("gpsimd", lambda e, i=idx: e.collective_compute(
            "AllGather", ALU.bypass, ins=[H["cc_in"][i].opt()],
            outs=[H["cc_out"][i].opt()], replica_groups=[list(range(8))]),
            kind="cc", reads=[f"ccin{idx}"], writes=[f"ccout{idx}"])
        E("gpsimd", lambda e, i=idx: e.dma_start(out=V["grow"][:, :], in_=H["cc_out"][i]),
          kind="d", reads=[f"ccout{idx}"], writes=["grow"])

    def global_max_finish(dst11, dstname):
        E("vector", lambda e, d=dst11: e.reduce_max(d, V["grow"][:, :], axis=AX.X),
          reads=["grow"], writes=[dstname], late=True)

    def global_max(idx, src11, srcname, dst11, dstname):
        global_max_launch(idx, src11, srcname)
        global_max_finish(dst11, dstname)

    def bcast(dst_col, dcolname, src11, srcname):
        E("tensor", lambda e, x=src11: e.matmul(
            paux[:, 4:5], V["onesf"][:, :], x, start=True, stop=True),
            reads=[srcname, "onesf"], writes=["paux"])
        E("vector", lambda e, d=dst_col: e.tensor_copy(d, paux[:, 4:5]),
          reads=["paux"], writes=[dcolname])

    def sr(i):
        return V["srow"][:, i:i + 1]

    def bc(i):
        return V["bcol"][:, i:i + 1]

    def scale_chain(gi, sig_i, s_i, inv_i, bc_inv, bc_s, half=False, half_i=None,
                    bc_half=None):
        E("vector", lambda e: e.tensor_tensor(sr(s_i), sr(gi), sr(sig_i), ALU.mult),
          reads=["srow"], writes=["srow"])
        E("vector", lambda e: e.tensor_scalar_mul(sr(s_i), sr(s_i), 1.0 / QMAX),
          reads=["srow"], writes=["srow"])
        E("vector", lambda e: e.reciprocal(sr(inv_i), sr(s_i)),
          reads=["srow"], writes=["srow"], late=True)
        bcast(bc(bc_inv), "bcol", sr(inv_i), "srow")
        bcast(bc(bc_s), "bcol", sr(s_i), "srow")
        if half:
            E("vector", lambda e: e.tensor_scalar_mul(sr(half_i), sr(s_i), 0.5),
              reads=["srow"], writes=["srow"])
            bcast(bc(bc_half), "bcol", sr(half_i), "srow")

    def quant_transpose(tc, xap, xname, guard=None, guard_reads=None):
        E("scalar", lambda e, x=xap, t=tc: e.activation(
            V["qscr"][:, :], x, AF.Identity,
            bias=V["BQ"][:, t:t + 1], scale=V["AQ"][:, t:t + 1]),
            reads=[xname, "AQ", "BQ"], writes=["qscr"])
        E("vector", lambda e: e.tensor_scalar(
            V["i32"][:, :], V["qscr"][:, :], 127.49, -128.49, ALU.min, ALU.max),
            reads=["qscr"], writes=["i32s"])
        E("vector", lambda e: e.tensor_copy(V["astage"][:, :], V["i32"][:, :]),
          reads=["i32s"], writes=["astage"])
        for c in range(DC):
            pt, ptn = (ptA, "ptA") if c % 2 == 0 else (ptB, "ptB")
            E("tensor", lambda e, c_=c, p=pt: e.transpose(
                p[:, 0:128], V["astage"][:, 128 * c_:128 * (c_ + 1)], V["ident_bf"][:, :]),
                reads=["astage", "identb"], writes=[ptn])
            rd = [ptn]
            if guard_reads and tc == 0 and c == 0:
                rd = rd + guard_reads
            E("scalar", lambda e, c_=c, t=tc, p=pt: e.activation(
                fmA(c_)[:, 128 * t:128 * (t + 1)], p[:, 0:128], AF.Identity),
                reads=rd, writes=[f"fmA:{c}"],
                extra=(guard if (guard and tc == 0 and c == 0) else ()))

    def emit_coop_prep():
        E("vector", lambda e: e.memset(V["wsg"][:, 6:8], 0.0), writes=["wsg"])
        for s in range(6):
            i, gu = s // 2, s % 2
            src = H["wg_mine"] if gu == 0 else H["wu_mine"]
            nat, natn = ((V["pnat0"], "pnat0") if s % 2 == 0
                         else (V["pnat1"], "pnat1"))
            stg, stgn = ((V["pstg0"], "pstg0") if s % 2 == 0
                         else (V["pstg1"], "pstg1"))
            E("sync", lambda e, i_=i, sc=src, o=nat: e.dma_start(
                out=o[:, :], in_=sc[128 * i_:128 * (i_ + 1), :]),
                kind="d", writes=[natn])
            E("vector", lambda e, n=nat: e.reduce_max(
                V["red1"][:, :], n[:, :], axis=AX.X, apply_absolute_value=True),
                reads=[natn], writes=["red1"], late=True)
            E("vector", lambda e: e.reciprocal(V["red2"][:, :], V["red1"][:, :]),
              reads=["red1"], writes=["red2"], late=True)
            E("vector", lambda e: e.tensor_scalar_mul(
                V["red2"][:, :], V["red2"][:, :], QMAX),
                reads=["red2"], writes=["red2"])
            E("vector", lambda e, n=nat: e.tensor_scalar(
                V["i32"][:, :], n[:, :], V["red2"][:, :], 127.49,
                ALU.mult, ALU.min), reads=[natn, "red2"], writes=["i32s"])
            E("vector", lambda e: e.tensor_scalar(
                V["pq"][:, :], V["i32"][:, :], -128.0, None, ALU.max),
                reads=["i32s"], writes=["pq"])
            E("vector", lambda e, s_=s: e.tensor_scalar_mul(
                V["wsg"][:, s_:s_ + 1], V["red1"][:, :], 1.0 / QMAX),
                reads=["red1"], writes=["wsg"])
            for k in range(DC):
                pt, ptn = (ptA, "ptA") if k % 2 == 0 else (ptB, "ptB")
                E("tensor", lambda e, k_=k, p=pt: e.transpose(
                    p[:, 0:128], V["pq"][:, 128 * k_:128 * (k_ + 1)],
                    V["ident_bf"][:, :]), reads=["pq", "identb"], writes=[ptn])
                E("scalar", lambda e, k_=k, p=pt, o=stg: e.activation(
                    o[:, 128 * k_:128 * (k_ + 1)], p[:, 0:128], AF.Identity),
                    reads=[ptn], writes=[stgn])
            E("gpsimd", lambda e, i_=i, g_=gu, o=stg: e.dma_start(
                out=H[f"myg{i_}"][128 * g_:128 * (g_ + 1), :], in_=o[:, :]),
                kind="d", reads=[stgn], writes=[f"myg:{s}"])
            if gu == 1:
                if i == 2:
                    E("gpsimd", lambda e: e.dma_start(out=H["mysc"],
                      in_=V["wsg"][:, 0:8]),
                      kind="d", reads=["wsg"], writes=["mysc"])
                    E("gpsimd", lambda e: e.collective_compute(
                        "AllGather", ALU.bypass, ins=[H["mysc"].opt()],
                        outs=[H["allsc"].opt()],
                        replica_groups=[list(range(8))]), kind="cc",
                        reads=["mysc"], writes=["ccoutsc"])
                E("gpsimd", lambda e, i_=i: e.collective_compute(
                    "AllGather", ALU.bypass, ins=[H[f"myg{i_}"].opt()],
                    outs=[H[f"allg{i_}"].opt()],
                    replica_groups=[list(range(8))]), kind="cc",
                    reads=[f"myg:{2 * i}", f"myg:{2 * i + 1}"],
                    writes=[f"ccoutg{i}"])
        gu_pe = [kb.mark("tensor")]
        E("sync", lambda e: e.dma_start(out=V["pwdn"][:, :], in_=H["wd_mine"]),
          kind="d", writes=["pwdn"])
        E("vector", lambda e: e.reduce_max(
            V["red1"][:, :], V["pwdn"][:, :], axis=AX.X,
            apply_absolute_value=True), reads=["pwdn"], writes=["red1"], late=True)
        E("vector", lambda e: e.reciprocal(V["red2"][:, :], V["red1"][:, :]),
          reads=["red1"], writes=["red2"], late=True)
        E("vector", lambda e: e.tensor_scalar_mul(V["red2"][:, :], V["red2"][:, :], QMAX),
          reads=["red2"], writes=["red2"])
        E("vector", lambda e: e.tensor_scalar_mul(V["red1"][:, :], V["red1"][:, :],
                                                  1.0 / QMAX),
          reads=["red1"], writes=["red1"])
        for hf in range(3):
            a, b = 1024 * hf, min(II, 1024 * (hf + 1))
            E("vector", lambda e, a_=a, b_=b: e.tensor_scalar(
                V["i32"][:, 0:b_ - a_], V["pwdn"][:, a_:b_], V["red2"][:, :], 127.49,
                ALU.mult, ALU.min),
                reads=["pwdn", "red2", "myg:4", "myg:5", "i32s"], writes=["i32s"],
                extra=(gu_pe if hf == 0 else ()))
            E("vector", lambda e, a_=a, b_=b: e.tensor_scalar(
                V["pwdq"][:, a_:b_], V["i32"][:, 0:b_ - a_], -128.0, V["red1"][:, :],
                ALU.max, ALU.mult), reads=["i32s", "red1"], writes=["pwdq"])
        gu_ve = [kb.mark("vector")]
        for j in range(JC):
            pt, ptn = (ptA, "ptA") if j % 2 == 0 else (ptB, "ptB")
            E("tensor", lambda e, j_=j, p=pt: e.transpose(
                p[:, 0:128], V["pwdq"][:, 128 * j_:128 * (j_ + 1)],
                V["ident_bf"][:, :]), reads=["pwdq", "identb"], writes=[ptn])
            E("scalar", lambda e, j_=j, p=pt: e.activation(
                V["pwdt"][:, 128 * j_:128 * (j_ + 1)], p[:, 0:128], AF.Identity),
                reads=[ptn], writes=["pwdt"], extra=gu_ve)
            E("gpsimd", lambda e, j_=j: e.dma_start(
                out=H["mywd"][128 * j_:128 * (j_ + 1), :],
                in_=V["pwdt"][:, 128 * j_:128 * (j_ + 1)]),
                kind="d", reads=["pwdt"], writes=[f"mywd:{j}"])
        E("gpsimd", lambda e: e.collective_compute(
            "AllGather", ALU.bypass, ins=[H["mywd"].opt()], outs=[H["allwd"].opt()],
            replica_groups=[list(range(8))]), kind="cc",
            reads=[f"mywd:{j}" for j in range(JC)], writes=["ccoutwd"])
        for c in range(8):
            E("sync", lambda e, c_=c: e.dma_start(
                out=V["wsga"][:, 8 * c_:8 * (c_ + 1)],
                in_=H["allsc"][128 * c_:128 * (c_ + 1), :]),
                kind="d", reads=["ccoutsc"], writes=["wsga"])
        return [kb.mark("tensor"), kb.mark("scalar")]

    def emit_wd_repack():
        for j in range(JC):
            for m in range(DC):
                E("gpsimd", lambda e, j_=j, m_=m: e.dma_start(
                    out=H["wdT_h"][128 * j_:128 * (j_ + 1), 128 * m_:128 * (m_ + 1)],
                    in_=H["allwd"][II * m_ + 128 * j_:II * m_ + 128 * (j_ + 1), :]),
                    kind="d", reads=["ccoutwd"], writes=[f"wdTh:{j}:{m}"])

    def emit_ffn(src_resident, src_hbm, dst_store, sig_idx, sgi, tag,
                 src_names=None, post_tile_hook=None, skip_stats=False,
                 prelaunched=False, quant_guard=None, quant_guard_reads=None):
        if not prelaunched:
            if not skip_stats:
                for tc in range(NT):
                    ln_stats(tc, xres_t(tc), f"x:{tc}", absmax=True)
            E("vector", lambda e: e.reduce_max(V["red2"][:, :], V["lnmax"][:, :],
                                               axis=AX.X),
              reads=["lnmax"], writes=["red2"], late=True)
            partition_max(V["red2"][:, :], "red2", sr(sgi), "srow")
            global_max_launch(0 if tag == "ffn1" else 2, sr(sgi), "srow")
        global_max_finish(sr(sgi + 1), "srow")
        scale_chain(sgi + 1, sig_idx, sgi + 2, sgi + 3, bc_inv=0, bc_s=1)
        E("vector", lambda e: e.tensor_scalar(
            V["AQ"][:, :], V["A_st"][:, :], bc(0), None, ALU.mult),
            reads=["A_st", "bcol"], writes=["AQ"])
        E("vector", lambda e: e.tensor_scalar(
            V["BQ"][:, :], V["B_st"][:, :], bc(0), None, ALU.mult),
            reads=["B_st", "bcol"], writes=["BQ"])
        for tc in range(NT):
            quant_transpose(tc, xres_t(tc), f"x:{tc}",
                            guard=quant_guard, guard_reads=quant_guard_reads)
        if os.environ.get("DBGF") and tag == "ffn1":
            E("gpsimd", lambda e: e.dma_start(out=H["out"][2036:2037, 0:128],
              in_=V["onesb"][:, :]), kind="d", reads=["onesb"], writes=["df0"])
        kb.barrier()

        E("vector", lambda e: e.tensor_scalar(
            V["bcol"][:, 16:80], V["wsga"][:, :], bc(1), None, ALU.mult),
            reads=["wsga", "bcol"], writes=["sgu"])

        def sgu_c(jc, gu):
            return 16 + 8 * (jc % 8) + 2 * (jc // 8) + gu

        if tag == "ffn1":
            emit_wd_repack()
        for j in (16, 17, 18, 19, 21):
            E("gpsimd", lambda e, j_=j: e.dma_start(
                out=wdT(j_)[:, :], in_=H["wdT_h"][128 * j_:128 * (j_ + 1), :]),
                kind="d", reads=[f"wdTh:{j}:{m}" for m in range(DC)],
                writes=[f"wdT:{j}"])

        for jc in range(JC):
            nb = jc % 2
            wgT, wuT = V[f"wgT{nb}"], V[f"wuT{nb}"]
            si, gr = jc // 8, 256 * (jc % 8)
            E("sync", lambda e, o=wgT, i_=si, r=gr: e.dma_start(
                out=o[:, :], in_=H[f"allg{i_}"][r:r + 128, :]),
                kind="d", reads=[f"ccoutg{si}"], writes=[f"wgT{nb}"])
            E("sync", lambda e, o=wuT, i_=si, r=gr: e.dma_start(
                out=o[:, :], in_=H[f"allg{i_}"][r + 128:r + 256, :]),
                kind="d", reads=[f"ccoutg{si}"], writes=[f"wuT{nb}"])
            for tt in range(NTT):
                bg, bgn = pm[(2 * tt) % 4], f"pm{(2 * tt) % 4}"
                bu, bun = pm[(2 * tt + 1) % 4], f"pm{(2 * tt + 1) % 4}"
                sl, sln = ((V["silu"], "silu") if tt % 2 == 0
                           else (V["silu2"], "silu2"))
                for kc in range(DC):
                    E("tensor", lambda e, k=kc, t=tt, b=bg, w=wgT: e.matmul(
                        b[:, :], w[:, 128 * k:128 * (k + 1)],
                        fmA(k)[:, 512 * t:512 * (t + 1)],
                        start=(k == 0), stop=(k == DC - 1)),
                        reads=[f"wgT{nb}", f"fmA:{kc}"], writes=[bgn])
                for kc in range(DC):
                    E("tensor", lambda e, k=kc, t=tt, b=bu, w=wuT: e.matmul(
                        b[:, :], w[:, 128 * k:128 * (k + 1)],
                        fmA(k)[:, 512 * t:512 * (t + 1)],
                        start=(k == 0), stop=(k == DC - 1)),
                        reads=[f"wuT{nb}", f"fmA:{kc}"], writes=[bun])
                cg, cu = sgu_c(jc, 0), sgu_c(jc, 1)
                E("scalar", lambda e, b=bg, c=cg, s_=sl: e.activation(
                    s_[:, :], b[:, :], AF.Silu,
                    scale=V["bcol"][:, c:c + 1]),
                    reads=[bgn, "sgu"], writes=[sln])
                E("vector", lambda e, b=bu, c=cu, j=jc, t=tt, s_=sl: e.scalar_tensor_tensor(
                    h_c(j)[:, 512 * t:512 * (t + 1)], b[:, :],
                    V["bcol"][:, c:c + 1], s_[:, :],
                    ALU.mult, ALU.mult),
                    reads=[bun, "sgu", sln], writes=[f"h:{jc}"])
                E("vector", lambda e, j=jc, t=tt: e.reduce_max(
                    V["habs"][:, 4 * j + t:4 * j + t + 1],
                    h_c(j)[:, 512 * t:512 * (t + 1)], axis=AX.X,
                    apply_absolute_value=True),
                    reads=[f"h:{jc}"], writes=["habs"], late=True)

        if os.environ.get("DBGF") and tag == "ffn1":
            E("gpsimd", lambda e: e.dma_start(out=H["out"][2037:2038, 0:128],
              in_=V["onesb"][:, :]), kind="d", reads=["onesb"], writes=["df1"])
        E("vector", lambda e: e.reduce_max(V["red2"][:, :], V["habs"][:, 0:88], axis=AX.X),
          reads=["habs"], writes=["red2"], late=True)
        partition_max(V["red2"][:, :], "red2", sr(sgi + 8), "srow")
        global_max_launch(1 if tag == "ffn1" else 3, sr(sgi + 8), "srow")

        alias_deps = [kb.mark("tensor"), kb.mark("vector"), kb.mark("scalar")]
        for qi, j in enumerate(j for j in range(JC) if j not in (16, 17, 18, 19, 21)):
            eng = "sync" if qi % 2 == 0 else "gpsimd"
            E(eng, lambda e, j_=j: e.dma_start(
                out=wdT(j_)[:, :], in_=H["wdT_h"][128 * j_:128 * (j_ + 1), :]),
                kind="d", reads=[f"wdTh:{j}:{m}" for m in range(DC)],
                writes=[f"wdT:{j}"], extra=alias_deps)

        global_max_finish(sr(sgi + 9), "srow")
        E("vector", lambda e: e.tensor_tensor(sr(63), sr(sgi + 9), sr(34),
                                              ALU.mult), reads=["srow"], writes=["srow"])
        E("vector", lambda e: e.tensor_scalar_mul(sr(sgi + 10), sr(63), 1.0 / QMAX),
          reads=["srow"], writes=["srow"])
        E("vector", lambda e: e.reciprocal(sr(sgi + 11), sr(sgi + 10)),
          reads=["srow"], writes=["srow"], late=True)
        E("vector", lambda e: e.tensor_scalar_mul(sr(sgi + 12), sr(sgi + 10), 0.5),
          reads=["srow"], writes=["srow"])
        bcast(bc(2), "bcol", sr(sgi + 11), "srow")
        bcast(bc(3), "bcol", sr(sgi + 12), "srow")

        kb.barrier()
        def quant_block(b, part=None):
            lo, hi = part if part else (0, JC)
            for jc in range(lo, hi):
                E("vector", lambda e, j=jc, b_=b: e.tensor_scalar(
                    V["i32h"][:, :], h_c(j)[:, 512 * b_:512 * (b_ + 1)],
                    bc(2), 127.49, ALU.mult, ALU.min),
                    reads=[f"h:{jc}", "bcol"], writes=["i32hs"])
                E("vector", lambda e, j=jc, b_=b: e.tensor_scalar(
                    h_c(j)[:, 512 * b_:512 * (b_ + 1)], V["i32h"][:, :],
                    -128.0, None, ALU.max),
                    reads=["i32hs"], writes=[f"h:{jc}"])

        quant_block(0)
        for tc in range(NT):
            xin, xinn = V["xin0"], "xin0"
            E("sync", lambda e, t=tc, o=xin: e.dma_start(
                out=o[:, :], in_=src_hbm[128 * t:128 * (t + 1), :]),
                kind="d", writes=[xinn],
                reads=[src_names.format(tc)] if src_names else [])
            for mb in range(2):
                b, bn = pm[2 * (tc % 2) + mb], f"pm{2 * (tc % 2) + mb}"
                for jc in range(JC):
                    E("tensor", lambda e, j=jc, t=tc, m=mb, b_=b: e.matmul(
                        b_[:, :], h_c(j)[:, 128 * t:128 * (t + 1)],
                        wdT(j)[:, 512 * m:512 * (m + 1)],
                        start=(j == 0), stop=(j == JC - 1)),
                        reads=[f"h:{jc}", f"wdT:{jc}"], writes=[bn])
                E("vector", lambda e, t=tc, m=mb, b_=b, xi=xin: e.scalar_tensor_tensor(
                    xres_t(t)[:, 512 * m:512 * (m + 1)], b_[:, :], bc(3),
                    xi[:, 512 * m:512 * (m + 1)], ALU.mult, ALU.add),
                    reads=[bn, "bcol", xinn], writes=[f"x:{tc}"])
            b_next = tc // 4 + 1
            if 4 * b_next < NT:
                qr = ((0, 6), (6, 12), (12, 18), (18, 22))[tc % 4]
                quant_block(b_next, qr)
            if dst_store is not None:
                E("gpsimd", lambda e, t=tc: e.dma_start(
                    out=dst_store[128 * t:128 * (t + 1), :], in_=xres_t(t)),
                    kind="d", reads=[f"x:{tc}"], writes=[f"xs:{tc}"])
            if post_tile_hook is not None:
                post_tile_hook(tc)
        kb.barrier()

    def emit_attn(dbg=None, skip_stats=False):
        E("sync", lambda e: e.dma_start(out=V["qscr"][0:1, :], in_=H["bvr"]),
          kind="d", writes=["qscr"])
        E("vector", lambda e: e.tensor_copy(V["bvrow"][:, :], V["qscr"][0:1, :]),
          reads=["qscr"], writes=["bvrow"])
        E("sync", lambda e: e.dma_start(out=V["qscr"][0:1, :], in_=H["bor"]),
          kind="d", writes=["qscr"])
        E("vector", lambda e: e.tensor_copy(V["borow"][:, :], V["qscr"][0:1, :]),
          reads=["qscr"], writes=["borow"])
        if dbg is not None:
            E("gpsimd", lambda e: e.dma_start(out=dbg[2039:2040, 0:128],
              in_=V["onesb"][:, :]), kind="d", reads=["onesb"], writes=["do0"])
            E("gpsimd", lambda e: e.dma_start(out=dbg[2030:2031, 0:1024],
              in_=V["dbgw"][0:1, 360:1384]), kind="d", reads=["onesb"], writes=["dw0"])
            E("gpsimd", lambda e: e.dma_start(out=dbg[2031:2032, 0:1024],
              in_=V["dbgw"][64:65, 360:1384]), kind="d", reads=["onesb"], writes=["dw1"])
            E("gpsimd", lambda e: e.dma_start(out=dbg[2032:2033, 0:1024],
              in_=V["dbgw"][1:2, 360:1384]), kind="d", reads=["onesb"], writes=["dw2"])
        for tc in range(NT):
            if not skip_stats:
                ln_stats(tc, xres_t(tc), f"x:{tc}")
            E("scalar", lambda e, t=tc: e.activation(
                V["qscr"][:, :], xres_t(t), AF.Identity,
                bias=V["B_st"][:, t:t + 1], scale=V["A_st"][:, t:t + 1]),
                reads=[f"x:{tc}", "A_st", "B_st"], writes=["qscr"])
            E("vector", lambda e: e.tensor_copy(V["astage"][:, :], V["qscr"][:, :]),
              reads=["qscr"], writes=["astage"])
            for c in range(DC):
                pt, ptn = (ptA, "ptA") if c % 2 == 0 else (ptB, "ptB")
                E("tensor", lambda e, c_=c, p=pt: e.transpose(
                    p[:, 0:128], V["astage"][:, 128 * c_:128 * (c_ + 1)],
                    V["ident_bf"][:, :]), reads=["astage", "identb"], writes=[ptn])
                E("vector", lambda e, c_=c, t=tc, p=pt: e.tensor_copy(
                    fmA(c_)[:, 128 * t:128 * (t + 1)], p[:, 0:128]),
                    reads=[ptn], writes=[f"fmA:{c}"])
        kb.barrier()

        def wprep(w_hbm, dstf, dsttag):
            def buf(m):
                return (((V["wnat_a"], "wnat_a") if m % 2 == 0
                         else (V["wnat_b"], "wnat_b")),
                        ((V["astage"], "astage") if m % 2 == 0
                         else (V["astage2"], "astage2")))

            def cast(m):
                (nat, natn), (stg, stgn) = buf(m)
                E("sync", lambda e, m_=m, o=nat: e.dma_start(
                    out=o[:, :], in_=w_hbm[128 * m_:128 * (m_ + 1), :]),
                    kind="d", writes=[natn])
                E("scalar", lambda e, n=nat, o=stg: e.activation(
                    o[:, :], n[:, :], AF.Identity),
                    reads=[natn], writes=[stgn])

            cast(0)
            for mc in range(DC):
                if mc + 1 < DC:
                    cast(mc + 1)
                (nat, natn), (stg, stgn) = buf(mc)
                for kcb in range(DC):
                    pt, ptn = (ptA, "ptA") if kcb % 2 == 0 else (ptB, "ptB")
                    E("tensor", lambda e, k=kcb, p=pt, s_=stg: e.transpose(
                        p[:, 0:128], s_[:, 128 * k:128 * (k + 1)],
                        V["ident_bf"][:, :]), reads=[stgn, "identb"], writes=[ptn])
                    if kcb % 2 == 0:
                        E("scalar", lambda e, k=kcb, m=mc, p=pt, d=dstf: e.activation(
                            d(k)[:, 128 * m:128 * (m + 1)], p[:, 0:128], AF.Identity),
                            reads=[ptn], writes=[f"{dsttag}:{kcb}"])
                    else:
                        E("vector", lambda e, k=kcb, m=mc, p=pt, d=dstf: e.tensor_copy(
                            d(k)[:, 128 * m:128 * (m + 1)], p[:, 0:128]),
                            reads=[ptn], writes=[f"{dsttag}:{kcb}"])

        if dbg is not None:
            E("gpsimd", lambda e: e.dma_start(out=dbg[2040:2041, 0:128],
              in_=V["onesb"][:, :]), kind="d", reads=["onesb"], writes=["do1"])
            E("gpsimd", lambda e: e.dma_start(out=dbg[2043:2044, 0:128],
              in_=V["onesf"][:, :]), kind="d", reads=["onesf"], writes=["do4"])
            E("gpsimd", lambda e: e.dma_start(out=dbg[2044:2045, 0:16],
              in_=V["A_st"][0:1, :]), kind="d", reads=["A_st"], writes=["do5"])
            E("gpsimd", lambda e: e.dma_start(out=dbg[2044:2045, 16:32],
              in_=V["B_st"][0:1, :]), kind="d", reads=["B_st"], writes=["do6"])
            E("gpsimd", lambda e: e.dma_start(out=dbg[2044:2045, 32:64],
              in_=V["ident_f32"][0:1, 0:32]), kind="d", reads=["identf"], writes=["do7"])
        wprep(H["wk"], wAc, "wA")
        if dbg is not None:
            E("gpsimd", lambda e: e.dma_start(out=dbg[2041:2042, 0:128],
              in_=V["onesb"][:, :]), kind="d", reads=["onesb"], writes=["do2"])
        wprep(H["wv"], wBc, "wB")
        if dbg is not None:
            E("gpsimd", lambda e: e.dma_start(out=dbg[2042:2043, 0:128],
              in_=V["onesb"][:, :]), kind="d", reads=["onesb"], writes=["do3"])

        SAFE = bool(os.environ.get("SAFEP1"))
        for tt in range(NTT):
            do_dump = (dbg is not None and tt == 0 and not os.environ.get("DBGP2"))
            if SAFE:
                kb.barrier()
            for mc in range(DC):
                b, bn = pm[mc % 2], f"pm{mc % 2}"
                for kc in range(DC):
                    E("tensor", lambda e, k=kc, m=mc, t=tt, b_=b: e.matmul(
                        b_[:, :], wAc(k)[:, 128 * m:128 * (m + 1)],
                        fmA(k)[:, 512 * t:512 * (t + 1)],
                        start=(k == 0), stop=(k == DC - 1)),
                        reads=[f"wA:{kc}", f"fmA:{kc}"], writes=[bn])
                E("scalar", lambda e, m=mc, b_=b: e.activation(
                    V["k_scr"][:, 512 * m:512 * (m + 1)], b_[:, :], AF.Identity,
                    bias=V["bk_sb"][:, m:m + 1]),
                    reads=[bn, "bk"], writes=[f"kscr:{mc}"])
            for c in range(DC):
                b, bn = pm[2 + c % 2], f"pm{2 + c % 2}"
                E("tensor", lambda e, c_=c, b_=b: e.matmul(
                    b_[0:64, :], V["pnT"][0:64, :],
                    V["k_scr"][0:64, 512 * c_:512 * (c_ + 1)], start=True, stop=True),
                    reads=[f"kscr:{c}", "pnT"], writes=[bn])
                E("tensor", lambda e, c_=c, b_=b: e.matmul(
                    b_[64:128, :], V["pnT"][64:128, :],
                    V["k_scr"][64:128, 512 * c_:512 * (c_ + 1)], start=True, stop=True),
                    reads=[f"kscr:{c}", "pnT"], writes=[bn])
                E("vector", lambda e, c_=c, b_=b: e.tensor_scalar(
                    V["kf_scr"][:, 512 * c_:512 * (c_ + 1)], b_[:, :], 0.0, EPS_AT,
                    ALU.max, ALU.add), reads=[bn], writes=[f"kf:{c}"])
            if do_dump:
                def dump4(srcv, srcnames, row0, tag):
                    for blk in range(4):
                        E("gpsimd", lambda e, b_=blk, s=srcv, r=row0: e.dma_start(
                            out=dbg[r + 128 * b_:r + 128 * (b_ + 1), :],
                            in_=s[:, 1024 * b_:1024 * (b_ + 1)]),
                            kind="d", reads=list(srcnames), writes=[f"{tag}{blk}"])
                dump4(V["k_scr"], [f"kscr:{i}" for i in range(DC)], 0, "dks")
            if SAFE:
                kb.barrier()
            if do_dump:
                if os.environ.get("DBGW"):
                    for blk in range(4):
                        E("gpsimd", lambda e, b_=blk: e.dma_start(
                            out=dbg[512 + 128 * b_:512 + 128 * (b_ + 1), :],
                            in_=V["wB"][:, 2048 * b_:2048 * (b_ + 1)][:, 0:1024]),
                            kind="d", reads=[f"wB:{2*blk}"], writes=[f"dwb{blk}"])
                    E("gpsimd", lambda e: e.dma_start(
                        out=dbg[2045:2046, 0:128], in_=V["onesb"][:, :]),
                        kind="d", reads=["onesb"], writes=["dob"])
                    E("gpsimd", lambda e: e.dma_start(
                        out=dbg[2046:2047, 0:1024], in_=V["bvrow"][:, :]),
                        kind="d", reads=["bvrow"], writes=["dbv"])
                else:
                    dump4(V["kf_scr"], [f"kf:{i}" for i in range(DC)], 512, "dkf")
            if SAFE:
                kb.barrier()
            for ts in range(4):
                pt, ptn = (ptA, "ptA") if ts % 2 == 0 else (ptB, "ptB")
                for c in range(DC):
                    E("tensor", lambda e, c_=c, t_=ts, p=pt: e.transpose(
                        p[:, 128 * c_:128 * (c_ + 1)],
                        V["kf_scr"][:, 512 * c_ + 128 * t_:512 * c_ + 128 * (t_ + 1)],
                        V["ident_bf"][:, :]),
                        reads=[f"kf:{c}", "identb"], writes=[ptn])
                E("scalar", lambda e, t_=ts, p=pt: e.activation(
                    V["kftm"][:, 1024 * t_:1024 * (t_ + 1)], p[:, :], AF.Identity),
                    reads=[ptn], writes=[f"kftm:{ts}"])
            if do_dump:
                dump4(V["kftm"], [f"kftm:{i}" for i in range(4)], 1024, "dkt")
            if SAFE:
                kb.barrier()
            for ts in range(4):
                tg = 4 * tt + ts
                for mb in range(2):
                    bi = (2 * ts + mb) % 4
                    b, bn = pm[bi], f"pm{bi}"
                    for kc in range(DC):
                        E("tensor", lambda e, k=kc, t_=tg, m=mb, b_=b: e.matmul(
                            b_[:, :], fmA(k)[:, 128 * t_:128 * (t_ + 1)],
                            wBc(k)[:, 512 * m:512 * (m + 1)],
                            start=(k == 0), stop=False),
                            reads=[f"wB:{kc}", f"fmA:{kc}"], writes=[bn])
                    E("tensor", lambda e, m=mb, b_=b: e.matmul(
                        b_[:, :], V["onesb"][:, :], V["bvrow"][:, 512 * m:512 * (m + 1)],
                        start=False, stop=True),
                        reads=["onesb", "bvrow"], writes=[bn])
                    E("scalar", lambda e, t_=ts, m=mb, b_=b: e.activation(
                        V["vtm"][:, 1024 * t_ + 512 * m:1024 * t_ + 512 * (m + 1)],
                        b_[:, :], AF.Identity), reads=[bn], writes=[f"vtm:{ts}"])
            if do_dump:
                for blk in range(3):
                    E("gpsimd", lambda e, b_=blk: e.dma_start(
                        out=dbg[1536 + 128 * b_:1536 + 128 * (b_ + 1), :],
                        in_=V["vtm"][:, 1024 * b_:1024 * (b_ + 1)]),
                        kind="d", reads=[f"vtm:{blk}"], writes=[f"dvt{blk}"])
            if SAFE:
                kb.barrier()
            for ts in range(4):
                for c in range(DC):
                    E("tensor", lambda e, c_=c, t_=ts,
                      f=(tt == 0 and ts == 0 and c == 0),
                      l=(tt == NTT - 1 and ts == 3 and c == DC - 1): e.matmul(
                        paux[:, 8 + c_:9 + c_],
                        V["kftm"][:, 1024 * t_ + 128 * c_:1024 * t_ + 128 * (c_ + 1)],
                        V["onecb"][:, :], start=f, stop=l),
                        reads=[f"kftm:{ts}", "onecb"], writes=["pksum"])
                    for hh in range(2):
                        first = (tt == 0 and ts == 0 and c == 0)
                        last = (tt == NTT - 1 and ts == 3)
                        rows = slice(64 * hh, 64 * (hh + 1))
                        E("tensor", lambda e, c_=c, t_=ts, h_=hh, r=rows,
                          f=first, l=last: e.matmul(
                            pkv[r, 64 * c_:64 * (c_ + 1)],
                            V["kftm"][:, 1024 * t_ + 128 * c_ + 64 * h_:
                                      1024 * t_ + 128 * c_ + 64 * (h_ + 1)],
                            V["vtm"][:, 1024 * t_ + 128 * c_ + 64 * h_:
                                     1024 * t_ + 128 * c_ + 64 * (h_ + 1)],
                            start=f, stop=l),
                            reads=[f"kftm:{ts}", f"vtm:{ts}"], writes=["pkv"])
        E("vector", lambda e: e.tensor_copy(V["kv_sb"][:, :], pkv[:, :]),
          reads=["pkv"], writes=["kv"])
        E("vector", lambda e: e.tensor_copy(V["ksum"][:, :], paux[:, 8:16]),
          reads=["pksum"], writes=["ksum"])
        if dbg is not None and not os.environ.get("DBGP2"):
            E("gpsimd", lambda e: e.dma_start(out=dbg[1920:2048, 0:512],
              in_=V["kv_sb"][:, :]), kind="d", reads=["kv"], writes=["dkv"])
            E("gpsimd", lambda e: e.dma_start(out=dbg[1920:2048, 512:520],
              in_=V["ksum"][:, :]), kind="d", reads=["ksum"], writes=["dksum"])
            return
        kb.barrier()

        wprep(H["wq"], wAc, "wA")
        wprep(H["wo"], wBc, "wB")
        E("vector", lambda e: e.memset(V["bd"][:, :], 0.0), writes=["bd"])
        for c in range(DC):
            E("vector", lambda e, c_=c: e.tensor_copy(
                V["bd"][0:64, 128 * c_:128 * c_ + 64],
                V["ksum"][0:64, c_:c_ + 1].broadcast_to((64, 64))),
                reads=["ksum"], writes=["bd"])
            E("vector", lambda e, c_=c: e.tensor_copy(
                V["bd"][64:128, 128 * c_ + 64:128 * (c_ + 1)],
                V["ksum"][64:128, c_:c_ + 1].broadcast_to((64, 64))),
                reads=["ksum"], writes=["bd"])
        kb.barrier()

        P2D = dbg is not None and os.environ.get("DBGP2")
        for tt in range(NTT):
            dump2 = P2D and tt == 0
            for mc in range(DC):
                b, bn = pm[mc % 2], f"pm{mc % 2}"
                for kc in range(DC):
                    E("tensor", lambda e, k=kc, m=mc, t=tt, b_=b: e.matmul(
                        b_[:, :], wAc(k)[:, 128 * m:128 * (m + 1)],
                        fmA(k)[:, 512 * t:512 * (t + 1)],
                        start=(k == 0), stop=(k == DC - 1)),
                        reads=[f"wA:{kc}", f"fmA:{kc}"], writes=[bn])
                E("scalar", lambda e, m=mc, b_=b: e.activation(
                    V["k_scr"][:, 512 * m:512 * (m + 1)], b_[:, :], AF.Identity,
                    bias=V["bq_sb"][:, m:m + 1]),
                    reads=[bn, "bq"], writes=[f"kscr:{mc}"])
            for c in range(DC):
                b, bn = pm[2 + c % 2], f"pm{2 + c % 2}"
                E("tensor", lambda e, c_=c, b_=b: e.matmul(
                    b_[0:64, :], V["pnT"][0:64, :],
                    V["k_scr"][0:64, 512 * c_:512 * (c_ + 1)], start=True, stop=True),
                    reads=[f"kscr:{c}", "pnT"], writes=[bn])
                E("tensor", lambda e, c_=c, b_=b: e.matmul(
                    b_[64:128, :], V["pnT"][64:128, :],
                    V["k_scr"][64:128, 512 * c_:512 * (c_ + 1)], start=True, stop=True),
                    reads=[f"kscr:{c}", "pnT"], writes=[bn])
                E("vector", lambda e, c_=c, b_=b: e.tensor_scalar(
                    V["kf_scr"][:, 512 * c_:512 * (c_ + 1)], b_[:, :], 0.0, EPS_AT,
                    ALU.max, ALU.add), reads=[bn], writes=[f"kf:{c}"])
            if dump2:
                for blk in range(4):
                    E("gpsimd", lambda e, b_=blk: e.dma_start(
                        out=dbg[1024 + 128 * b_:1024 + 128 * (b_ + 1), :],
                        in_=V["k_scr"][:, 1024 * b_:1024 * (b_ + 1)]),
                        kind="d", reads=[f"kscr:{2*blk}", f"kscr:{2*blk+1}"], writes=[f"dq{blk}"])
                for blk in range(4):
                    E("gpsimd", lambda e, b_=blk: e.dma_start(
                        out=dbg[512 + 128 * b_:512 + 128 * (b_ + 1), :],
                        in_=V["kf_scr"][:, 1024 * b_:1024 * (b_ + 1)]),
                        kind="d", reads=[f"kf:{2*blk}", f"kf:{2*blk+1}"], writes=[f"dqf{blk}"])
                E("gpsimd", lambda e: e.dma_start(
                    out=dbg[1664:1792, :], in_=V["bd"][:, 0:1024]),
                    kind="d", reads=["bd"], writes=["dbd"])
                E("gpsimd", lambda e: e.dma_start(
                    out=dbg[1792:1920, 0:512], in_=V["kv_sb"][:, :]),
                    kind="d", reads=["kv"], writes=["dkv2"])
            for c in range(DC):
                bdn, bdnn = pm[c % 2], f"pm{c % 2}"
                E("tensor", lambda e, c_=c, b_=bdn: e.matmul(
                    b_[:, :], V["bd"][:, 128 * c_:128 * (c_ + 1)],
                    V["kf_scr"][:, 512 * c_:512 * (c_ + 1)], start=True, stop=True),
                    reads=["bd", f"kf:{c}"], writes=[bdnn])
                if dump2 and c == 0:
                    E("vector", lambda e, b_=bdn: e.tensor_copy(
                        V["qscr"][:, 0:512], b_[:, :]), reads=[bdnn], writes=["qscr"])
                    E("gpsimd", lambda e: e.dma_start(
                        out=dbg[1536:1664, 0:512], in_=V["qscr"][:, 0:512]),
                        kind="d", reads=["qscr"], writes=["dden"])
                rd, rdn = ((V["rden"], "rden") if c % 2 == 0
                           else (V["rdenb"], "rdenb"))
                E("vector", lambda e, b_=bdn, r=rd: e.tensor_scalar(
                    r[:, :], b_[:, :], EPS_AT, None, ALU.add),
                    reads=[bdnn], writes=[rdn])
                E("vector", lambda e, r=rd: e.reciprocal(r[:, :], r[:, :]),
                  reads=[rdn], writes=[rdn], late=True)
                bq_, bqn = pm[2 + c % 2], f"pm{2 + c % 2}"
                E("tensor", lambda e, c_=c, b_=bq_: e.matmul(
                    b_[0:64, :], V["kv_sb"][0:64, 64 * c_:64 * (c_ + 1)],
                    V["kf_scr"][0:64, 512 * c_:512 * (c_ + 1)], start=True, stop=True),
                    reads=["kv", f"kf:{c}"], writes=[bqn])
                E("tensor", lambda e, c_=c, b_=bq_: e.matmul(
                    b_[64:128, :], V["kv_sb"][64:128, 64 * c_:64 * (c_ + 1)],
                    V["kf_scr"][64:128, 512 * c_:512 * (c_ + 1)], start=True, stop=True),
                    reads=["kv", f"kf:{c}"], writes=[bqn])
                if dump2 and c == 0:
                    E("gpsimd", lambda e: e.dma_start(
                        out=dbg[1536:1664, 512:1024], in_=V["rden"][:, :]),
                        kind="d", reads=["rden"], writes=["drden"])
                    E("vector", lambda e, b_=bq_: e.tensor_copy(
                        V["qscr"][:, 0:512], b_[:, :]), reads=[bqn], writes=["qscr"])
                    E("gpsimd", lambda e: e.dma_start(
                        out=dbg[1920:2048, 512:1024], in_=V["qscr"][:, 0:512]),
                        kind="d", reads=["qscr"], writes=["dnum"])
                E("vector", lambda e, c_=c, b_=bq_, r=rd: e.tensor_tensor(
                    V["attn_scr"][:, 512 * c_:512 * (c_ + 1)], b_[:, :], r[:, :],
                    ALU.mult), reads=[bqn, rdn], writes=[f"at:{c}"])
            for ts in range(4):
                tg = 4 * tt + ts
                for mb in range(2):
                    bi = (2 * ts + mb) % 4
                    b, bn = pm[bi], f"pm{bi}"
                    for c in range(DC):
                        E("tensor", lambda e, c_=c, t_=ts, m=mb, b_=b: e.matmul(
                            b_[:, :],
                            V["attn_scr"][:, 512 * c_ + 128 * t_:512 * c_ + 128 * (t_ + 1)],
                            wBc(c_)[:, 512 * m:512 * (m + 1)],
                            start=(c_ == 0), stop=False),
                            reads=[f"at:{c}", f"wB:{c}"], writes=[bn])
                    E("tensor", lambda e, m=mb, b_=b: e.matmul(
                        b_[:, :], V["onesb"][:, :], V["borow"][:, 512 * m:512 * (m + 1)],
                        start=False, stop=True),
                        reads=["onesb", "borow"], writes=[bn])
                    E("vector", lambda e, t_=tg, m=mb, b_=b: e.tensor_add(
                        xres_t(t_)[:, 512 * m:512 * (m + 1)], b_[:, :],
                        xres_t(t_)[:, 512 * m:512 * (m + 1)]),
                        reads=[bn, f"x:{tg}"], writes=[f"x:{tg}"])
                E("gpsimd", lambda e, t_=tg: e.dma_start(
                    out=H["x2s"][128 * t_:128 * (t_ + 1), :], in_=xres_t(t_)),
                    kind="d", reads=[f"x:{tg}"], writes=[f"x2s:{tg}"])
                if dbg is None:
                    ln_stats(tg, xres_t(tg), f"x:{tg}", absmax=True)
        kb.barrier()

    E("scalar", lambda e: e.activation(sr(18), V["clip_sb"][:, 0:1], AF.Sigmoid),
      reads=["clip"], writes=["srow"])
    E("scalar", lambda e: e.activation(sr(34), V["clip_sb"][:, 1:2], AF.Sigmoid),
      reads=["clip2"], writes=["srow"])

    if PHASE == "dbg1":
        for tc in range(NT):
            ln_stats(tc, xres_t(tc), f"x:{tc}", absmax=True)
        E("vector", lambda e: e.reduce_max(V["red2"][:, :], V["lnmax"][:, :], axis=AX.X),
          reads=["lnmax"], writes=["red2"], late=True)
        partition_max(V["red2"][:, :], "red2", sr(2), "srow")
        global_max(0, sr(2), "srow", sr(3), "srow")
        scale_chain(3, 18, 4, 5, bc_inv=0, bc_s=1)
        E("vector", lambda e: e.tensor_scalar(
            V["AQ"][:, :], V["A_st"][:, :], bc(0), None, ALU.mult),
            reads=["A_st", "bcol"], writes=["AQ"])
        E("vector", lambda e: e.tensor_scalar(
            V["BQ"][:, :], V["B_st"][:, :], bc(0), None, ALU.mult),
            reads=["B_st", "bcol"], writes=["BQ"])
        E("gpsimd", lambda e: e.dma_start(out=H["out"][0:128, 0:16], in_=V["A_st"][:, :]),
          kind="d", reads=["A_st"], writes=["o1"])
        E("gpsimd", lambda e: e.dma_start(out=H["out"][0:128, 16:32], in_=V["B_st"][:, :]),
          kind="d", reads=["B_st"], writes=["o2"])
        E("gpsimd", lambda e: e.dma_start(out=H["out"][0:128, 32:48], in_=V["AQ"][:, :]),
          kind="d", reads=["AQ"], writes=["o3"])
        E("gpsimd", lambda e: e.dma_start(out=H["out"][0:128, 48:64], in_=V["BQ"][:, :]),
          kind="d", reads=["BQ"], writes=["o4"])
        E("gpsimd", lambda e: e.dma_start(out=H["out"][0:128, 64:80], in_=V["lnmax"][:, :]),
          kind="d", reads=["lnmax"], writes=["o5"])
        E("gpsimd", lambda e: e.dma_start(out=H["out"][300:301, 0:64], in_=V["srow"][:, :]),
          kind="d", reads=["srow"], writes=["o6"])
        E("gpsimd", lambda e: e.dma_start(out=H["out"][301:302, 0:8], in_=V["grow"][:, :]),
          kind="d", reads=["grow"], writes=["o7"])
        E("gpsimd", lambda e: e.dma_start(out=H["out"][310:438, 0:32], in_=V["bcol"][:, :]),
          kind="d", reads=["bcol"], writes=["o8"])
        E("gpsimd", lambda e: e.dma_start(out=H["out"][302:303, 0:2], in_=V["clip_sb"][:, :]),
          kind="d", reads=["clip", "clip2"], writes=["o9"])
        return

    if PHASE == "xq1":
        for tc in range(NT):
            ln_stats(tc, xres_t(tc), f"x:{tc}", absmax=True)
        E("vector", lambda e: e.reduce_max(V["red2"][:, :], V["lnmax"][:, :], axis=AX.X),
          reads=["lnmax"], writes=["red2"], late=True)
        partition_max(V["red2"][:, :], "red2", sr(2), "srow")
        global_max(0, sr(2), "srow", sr(3), "srow")
        scale_chain(3, 18, 4, 5, bc_inv=0, bc_s=1)
        E("vector", lambda e: e.tensor_scalar(
            V["AQ"][:, :], V["A_st"][:, :], bc(0), None, ALU.mult),
            reads=["A_st", "bcol"], writes=["AQ"])
        E("vector", lambda e: e.tensor_scalar(
            V["BQ"][:, :], V["B_st"][:, :], bc(0), None, ALU.mult),
            reads=["B_st", "bcol"], writes=["BQ"])
        for tc in range(NT):
            E("scalar", lambda e, t=tc: e.activation(
                V["qscr"][:, :], xres_t(t), AF.Identity,
                bias=V["BQ"][:, t:t + 1], scale=V["AQ"][:, t:t + 1]),
                reads=[f"x:{tc}", "AQ", "BQ"], writes=["qscr"])
            E("vector", lambda e: e.tensor_scalar(
                V["i32"][:, :], V["qscr"][:, :], 127.49, -128.49, ALU.min, ALU.max),
                reads=["qscr"], writes=["i32s"])
            E("vector", lambda e: e.tensor_copy(V["qscr"][:, :], V["i32"][:, :]),
              reads=["i32s"], writes=["qscr"])
            E("gpsimd", lambda e, t=tc: e.dma_start(
                out=H["out"][128 * t:128 * (t + 1), :], in_=V["qscr"][:, :]),
                kind="d", reads=["qscr"], writes=[f"o:{tc}"])
        return

    dst1 = H["out"] if PHASE == "x1" else None

    def ln2_hook(tc):
        if PHASE not in ("x1",):
            ln_stats(tc, xres_t(tc), f"x:{tc}")

    for tc in range(NT):
        ln_stats(tc, xres_t(tc), f"x:{tc}", absmax=True)
    E("vector", lambda e: e.reduce_max(V["red2"][:, :], V["lnmax"][:, :], axis=AX.X),
      reads=["lnmax"], writes=["red2"], late=True)
    partition_max(V["red2"][:, :], "red2", sr(2), "srow")
    global_max_launch(0, sr(2), "srow")
    prep_marks = emit_coop_prep()
    prep_reads = ([f"myg:{s}" for s in range(6)] + ["mysc"]
                  + [f"mywd:{j}" for j in range(JC)])
    emit_ffn(True, H["x_in"], dst1, sig_idx=18, sgi=2, tag="ffn1",
             post_tile_hook=ln2_hook, prelaunched=True,
             quant_guard=prep_marks, quant_guard_reads=prep_reads)
    if PHASE == "x1":
        E("gpsimd", lambda e: e.dma_start(out=H["dbg"][0:1, :], in_=V["srow"][:, :]),
          kind="d", reads=["srow"], writes=["dbg"])
        E("gpsimd", lambda e: e.dma_start(out=H["dbg"][1:2, 0:32], in_=V["bcol"][0:1, 0:32]),
          kind="d", reads=["bcol"], writes=["dbg"])
        E("gpsimd", lambda e: e.dma_start(out=H["dbg"][1:2, 32:64], in_=V["onesb"][:, 0:32]),
          kind="d", reads=["onesb"], writes=["dbg"])
        return
    kb.barrier()

    if PHASE == "dbga":
        emit_attn(dbg=H["out"] if not os.environ.get("DBGP2") else None)
        return
    if PHASE == "dbga2":
        emit_attn(dbg=H["out"])
        return

    emit_attn(skip_stats=(PHASE not in ("x1",)))
    if PHASE == "x2":
        for tc in range(NT):
            E("gpsimd", lambda e, t=tc: e.dma_start(
                out=H["out"][128 * t:128 * (t + 1), :], in_=xres_t(t)),
                kind="d", reads=[f"x:{tc}"], writes=[f"o:{tc}"])
        return

    emit_ffn(True, H["x2s"], H["out"], sig_idx=18, sgi=40, tag="ffn3",
             src_names="x2s:{}", skip_stats=True)
    E("gpsimd", lambda e: e.dma_start(out=H["dbg"][0:1, :], in_=V["srow"][:, :]),
      kind="d", reads=["srow"], writes=["dbg"])
    E("gpsimd", lambda e: e.dma_start(out=H["dbg"][1:2, 0:32], in_=V["bcol"][0:1, 0:32]),
      kind="d", reads=["bcol"], writes=["dbg"])



def kernel(**inputs):
    x = np.ascontiguousarray(np.asarray(inputs["L0_attn_x"], np.float32))
    B = x.shape[0]

    def bias_cols(b):
        return np.ascontiguousarray(np.asarray(b, np.float32).reshape(DC, 128).T)

    common = {
        "wq": np.ascontiguousarray(np.asarray(inputs["L0_attn_Wq"], np.float32)),
        "wk": np.ascontiguousarray(np.asarray(inputs["L0_attn_Wk"], np.float32)),
        "wv": np.ascontiguousarray(np.asarray(inputs["L0_attn_Wv"], np.float32)),
        "wo": np.ascontiguousarray(np.asarray(inputs["L0_attn_Wo"], np.float32)),
        "bq": bias_cols(inputs["L0_attn_bq"]),
        "bk": bias_cols(inputs["L0_attn_bk"]),
        "bvr": np.asarray(inputs["L0_attn_bv"], np.float32).reshape(1, D).copy(),
        "bor": np.asarray(inputs["L0_attn_bo"], np.float32).reshape(1, D).copy(),
        "pT": np.ascontiguousarray(np.asarray(inputs["L0_attn_P"], np.float32).T),
        "clip_x": np.asarray(inputs["L0_ffn_clip_x"], np.float32).reshape(1, 1).copy(),
        "clip_down": np.asarray(inputs["L0_ffn_clip_down"], np.float32).reshape(1, 1).copy(),
    }
    wg = np.asarray(inputs["L0_ffn_w_gate"], np.float32)
    wu = np.asarray(inputs["L0_ffn_w_up"], np.float32)
    wd = np.asarray(inputs["L0_ffn_w_down"], np.float32)
    nc = build_program()
    in_maps = []
    for b in range(NCORES):
        mm = dict(common)
        mm["x"] = np.ascontiguousarray(x[b % B])
        jl = [b, 8 + b, min(16 + b, JC - 1)]
        mm["wg_mine"] = np.ascontiguousarray(
            np.concatenate([wg[128 * j:128 * (j + 1)] for j in jl], axis=0))
        mm["wu_mine"] = np.ascontiguousarray(
            np.concatenate([wu[128 * j:128 * (j + 1)] for j in jl], axis=0))
        mm["wd_mine"] = np.ascontiguousarray(wd[128 * b:128 * (b + 1)])
        in_maps.append(mm)
    res = run_bass_kernel_spmd(nc, in_maps, core_ids=list(range(NCORES)))
    out = np.stack([res.results[b]["out"] for b in range(min(B, NCORES))], axis=0)
    try:
        kernel.dbg = np.stack([res.results[b]["dbg"] for b in range(NCORES)], axis=0)
    except Exception:
        kernel.dbg = None
    return out.astype(np.float32)
